# version 07
# baseline (speedup 1.0000x reference)
import numpy as np
import ml_dtypes
import concourse.bass as bass
import concourse.mybir as mybir
from concourse.bass_utils import run_bass_kernel_spmd

F32 = mybir.dt.float32
BF16 = mybir.dt.bfloat16
I32 = mybir.dt.int32
AF = mybir.ActivationFunctionType
ALU = mybir.AluOpType
AX = mybir.AxisListType

NCORES = 8
D_MODEL = 1024
D_INNER = 2048
N_ST = 16
L = 16384
I_FFN = 2816
TOK = L // NCORES
DI = D_INNER // NCORES
NTT = TOK // 128
NIT = I_FFN // 128
EPS = 1e-5
BF = np.dtype(ml_dtypes.bfloat16)
C6 = 1.0 / 6.0

IGR = I_FFN // 8
DGR = D_MODEL // 8
SZ_G = IGR * D_MODEL
SZ_D = DGR * I_FFN
WQ_ROW = 2 * SZ_G + SZ_D


class Prog:
    ENGS = ("sync", "scalar", "vector", "tensor", "gpsimd")

    def __init__(self):
        self.q = {e: [] for e in self.ENGS}
        self.cnt = {}

    def on(self, eng, f):
        self.q[eng].append(f)

    def inc(self, sem, k=1):
        self.cnt[sem] = self.cnt.get(sem, 0) + k
        return self.cnt[sem]

    def val(self, sem):
        return self.cnt.get(sem, 0)


def build(stage="full", dev=False, p1n=None, skip=()):
    nc = bass.Bass("TRN2", target_bir_lowering=False, debug=False)
    P = Prog()

    def ein(name, shape, dt):
        return nc.dram_tensor(name, shape, dt, kind="ExternalInput").ap()

    def hbm(name, shape, dt):
        return nc.dram_tensor(name, shape, dt, kind="Internal")

    x_in = ein("x_in", [TOK, D_MODEL], F32)
    ident_in = ein("ident_in", [128, 128], F32)
    mask_in = ein("mask_in", [128, 3], I32)
    fc_in = ein("fc_in", [128, 4], F32)
    wg_sl = ein("wg_sl", [IGR, D_MODEL], F32)
    w_inT_sl = ein("w_inT_sl", [D_MODEL, 4 * 128], F32)
    xpw_sl = ein("xpw_sl", [DI, 33], F32)
    conv_sl = ein("conv_sl", [DI, 4], F32)
    convb_sl = ein("convb_sl", [128, 2], F32)
    d_sl = ein("d_sl", [128, 2], F32)
    dtw_sl = ein("dtw_sl", [128, 2], F32)
    dtb_sl = ein("dtb_sl", [128, 2], F32)
    alog_sl = ein("alog_sl", [DI, N_ST], F32)
    w_outT_in = ein("w_outT_in", [D_INNER, D_MODEL], F32)
    wu_sl = ein("wu_sl", [IGR, D_MODEL], F32)
    wd_sl = ein("wd_sl", [DGR, I_FFN], F32)

    out = nc.dram_tensor("out", [TOK, D_MODEL], F32, kind="ExternalOutput").ap()
    dv = {}
    if dev:
        for nm, sh, dt in [("xn_dev", [TOK, D_MODEL], F32),
                           ("xnq_dev", [TOK, D_MODEL], F32),
                           ("hq_dev", [TOK, I_FFN], F32),
                           ("gate_dev", [128, I_FFN], F32),
                           ("g_dev", [TOK, D_MODEL], F32)]:
            dv[nm] = nc.dram_tensor(nm, sh, dt, kind="ExternalOutput").ap()

    wq_ag_in = hbm("wq_ag_in", [1, WQ_ROW], BF16)
    wq_ag_out = hbm("wq_ag_out", [NCORES, WQ_ROW], BF16)
    xnT_ag_in = hbm("xnT_ag_in", [D_MODEL, TOK], BF16)
    xnT_ag_out = hbm("xnT_ag_out", [NCORES * D_MODEL, TOK], BF16)
    hq_scr = hbm("hq_scr", [TOK, I_FFN], BF16)
    g_scr = hbm("g_scr", [TOK, D_MODEL], F32)
    xc_scr = hbm("xc_scr", [DI, L], F32)
    z_scr = hbm("z_scr", [DI, L], F32)
    xp_ar_in = hbm("xp_ar_in", [33, L], F32)
    xp_ar_out = hbm("xp_ar_out", [33, L], F32)
    dtr_rep_h = hbm("dtr_rep_h", [128, L], F32)
    yz_ag_in = hbm("yz_ag_in", [DI, L], BF16)
    yz_ag_out = hbm("yz_ag_out", [D_INNER, L], BF16)

    ARENA = 200 * 1024
    arena = nc.alloc_sbuf_tensor("myarena", [128, ARENA], mybir.dt.uint8)
    SB_BASE = nc.lookup_mloc(arena).addr
    ptr = [SB_BASE]

    def sb(name, shape, dt, at=None):
        per = int(np.prod(shape[1:])) * mybir.dt.size(dt)
        per = (per + 31) & ~31
        if at is None:
            at = ptr[0]
            ptr[0] = at + per
        assert at + per <= SB_BASE + ARENA, (name, at - SB_BASE, per)
        return nc.alloc_sbuf_tensor_at(name, shape, dt, offset=at)

    ident = sb("ident", [128, 128], F32)
    identb = sb("identb", [128, 128], BF16)
    maskc = sb("maskc", [128, 3], I32)
    fconst = sb("fconst", [128, 4], F32)
    amax_t = sb("amax_t", [128, 176], F32)
    sc_t = sb("sc_t", [128, 176], F32)
    rs_t = sb("rs_t", [128, 176], F32)
    hs_t = sb("hs_t", [128, 176], F32)
    ln_a = sb("ln_a", [128, 1], F32)
    ln_b = sb("ln_b", [128, 1], F32)
    ln_c = sb("ln_c", [128, 1], F32)
    ln_d = sb("ln_d", [128, 1], F32)
    PERM_END = ptr[0]

    xt_a_off = ptr[0]
    xt_a = sb("xt_a", [128, D_MODEL], F32)
    xt_b = sb("xt_b", [128, D_MODEL], F32)
    xn_a = sb("xn_a", [128, D_MODEL], F32)
    q_u = sb("q_u", [128, I_FFN], F32)
    q_t1 = sb("q_t1", [128, I_FFN], F32)
    q_t2 = sb("q_t2", [128, I_FFN], F32)
    q_t3 = sb("q_t3", [128, I_FFN], F32)
    xq_a = sb("xq_a", [128, D_MODEL], BF16)
    xnb_a = sb("xnb_a", [128, D_MODEL], BF16)
    evac_a = sb("evac_a", [128, 128], BF16)
    evac_b = sb("evac_b", [128, 128], BF16)
    xnqT = sb("xnqT", [128, 8 * TOK], BF16)
    wqs_b = sb("wqs_b", [128, I_FFN], BF16)
    FW = ptr[0]
    wgT = sb("wgT", [128, 8 * I_FFN], BF16)
    wuT = sb("wuT", [128, 8 * I_FFN], BF16)
    ffn_t1 = sb("ffn_t1", [128, 1024], F32)
    hq_a = sb("hq_a", [128, I_FFN], BF16)
    A_END = ptr[0]
    wsl_a = sb("wsl_a", [128, I_FFN], F32, at=FW)
    hu_b = sb("hu_b", [128, I_FFN], F32, at=SB_BASE + PERM_END - SB_BASE if False else xt_a_off)
    wdT = sb("wdT", [128, NIT * D_MODEL], BF16, at=FW)
    hqT_b = sb("hqT_b", [128, NIT * 128], BF16, at=FW + NIT * D_MODEL * 2)
    hq_in = sb("hq_in", [128, I_FFN], BF16, at=FW + NIT * D_MODEL * 2 + NIT * 256)
    gout_b = sb("gout_b", [128, D_MODEL], F32,
                at=FW + NIT * D_MODEL * 2 + NIT * 256 + I_FFN * 2)

    ps_big = nc.alloc_psum_tensor("ps_big", [128, 1024], F32)
    ps_a = nc.alloc_psum_tensor("ps_a", [128, 512], F32)
    ps_b = nc.alloc_psum_tensor("ps_b", [128, 512], F32)
    ps_t1 = nc.alloc_psum_tensor("ps_t1", [128, 128], BF16)
    ps_t2 = nc.alloc_psum_tensor("ps_t2", [128, 128], BF16)
    ps_xp = nc.alloc_psum_tensor("ps_xp", [33, 512], F32)
    ps_sp = nc.alloc_psum_tensor("ps_sp", [128, 512], F32)

    import contextlib
    ctx = contextlib.ExitStack()
    sem = {}
    for s in ("LD", "ACTS", "VEC", "TEN", "POOL", "ST", "CC", "LD2", "LD3"):
        sem[s] = ctx.enter_context(nc.semaphore(s))
    LD, ACTS, VEC, TEN, POOL, ST, CC, LD2, LD3 = (sem[s] for s in
        ("LD", "ACTS", "VEC", "TEN", "POOL", "ST", "CC", "LD2", "LD3"))

    def quant(v, src, dst, W, ub=None):
        ub = ub if ub is not None else q_u
        nb = W // 16
        prow_s = src.ap().ap[0][0]
        av = bass.AP(src.ap().tensor, 0, [[prow_s, 128], [16, nb], [1, 16]])
        v.tensor_reduce(amax_t[:, :nb], av, AX.X, ALU.max, apply_absolute_value=True)
        v.drain()
        v.tensor_scalar(sc_t[:, :nb], amax_t[:, :nb], C6, 1e-12, ALU.mult, ALU.max)
        v.drain()
        v.reciprocal(rs_t[:, :nb], sc_t[:, :nb])
        v.tensor_scalar(hs_t[:, :nb], sc_t[:, :nb], 0.5, None, ALU.mult)
        v.drain()
        rsb = bass.AP(rs_t.ap().tensor, 0, [[rs_t.ap().ap[0][0], 128], [1, nb], [0, 16]])
        uv = bass.AP(ub.ap().tensor, 0, [[ub.ap().ap[0][0], 128], [16, nb], [1, 16]])
        v.tensor_tensor(uv, av, rsb, ALU.mult)
        ui = ub[:, :W].bitcast(I32)
        t1i = q_t1[:, :W].bitcast(I32)
        v.tensor_scalar(t1i, ui, maskc[:, 0:1], None, ALU.bitwise_and)
        uu = q_t1[:, :W]
        v.tensor_scalar(q_t2[:, :W], uu, 0.25, None, ALU.is_gt)
        v.scalar_tensor_tensor(q_t2[:, :W], uu, 0.75, q_t2[:, :W], ALU.is_gt, ALU.add)
        v.scalar_tensor_tensor(q_t2[:, :W], uu, 1.25, q_t2[:, :W], ALU.is_gt, ALU.add)
        v.scalar_tensor_tensor(q_t2[:, :W], uu, 1.75, q_t2[:, :W], ALU.is_gt, ALU.add)
        v.tensor_scalar(q_t3[:, :W], uu, 2.5, None, ALU.is_gt)
        v.scalar_tensor_tensor(q_t3[:, :W], uu, 3.5, q_t3[:, :W], ALU.is_gt, ALU.add)
        v.tensor_scalar(uu, uu, 5.0, 4.0, ALU.is_gt, ALU.mult)
        v.scalar_tensor_tensor(q_t3[:, :W], q_t3[:, :W], 2.0, uu, ALU.mult, ALU.add)
        v.tensor_tensor(q_t2[:, :W], q_t2[:, :W], q_t3[:, :W], ALU.add)
        t3i = q_t3[:, :W].bitcast(I32)
        v.tensor_scalar(t3i, ui, maskc[:, 1:2], maskc[:, 2:3],
                        ALU.bitwise_and, ALU.bitwise_or)
        v.tensor_tensor(q_t2[:, :W], q_t2[:, :W], q_t3[:, :W], ALU.mult)
        hsb = bass.AP(hs_t.ap().tensor, 0, [[hs_t.ap().ap[0][0], 128], [1, nb], [0, 16]])
        qv = bass.AP(q_t2.ap().tensor, 0, [[q_t2.ap().ap[0][0], 128], [16, nb], [1, 16]])
        dv3 = bass.AP(dst.ap().tensor, 0, [[dst.ap().ap[0][0], 128], [16, nb], [1, 16]])
        v.tensor_tensor(dv3, qv, hsb, ALU.mult)

    def ld_const(e):
        e.dma_start(out=ident[:, :], in_=ident_in).then_inc(LD, 16)
        e.dma_start(out=maskc[:, :], in_=mask_in).then_inc(LD, 16)
        e.dma_start(out=fconst[:, :], in_=fc_in).then_inc(LD, 16)
    P.on("sync", ld_const)
    P.inc("LD", 48)

    def cast_ident(v):
        v.wait_ge(LD, 48)
        v.tensor_copy(identb[:, :], ident[:, :]).then_inc(VEC, 1)
    P.on("vector", cast_ident)
    v_ident = P.inc("VEC", 1)

    def wslice(src_ap, rows, width, dst_off):
        for it in range((rows + 127) // 128):
            r0, r1 = it * 128, min((it + 1) * 128, rows)
            rr = r1 - r0
            prev_vec = P.val("VEC")

            def ldw(e, r0=r0, r1=r1, width=width, prev_vec=prev_vec):
                e.wait_ge(VEC, prev_vec)
                e.dma_start(out=wsl_a[:r1 - r0, :width],
                            in_=src_ap[r0:r1, :]).then_inc(LD, 16)
            P.on("sync", ldw)
            ldv = P.inc("LD", 16)

            def qw(v, rr=rr, width=width, ldv=ldv):
                v.wait_ge(LD, ldv)
                quant(v, wsl_a, wqs_b, width)
                v.engine_nop().then_inc(VEC, 1)
            P.on("vector", qw)
            qv = P.inc("VEC", 1)

            def stw(g, r0=r0, rr=rr, width=width, qv=qv, dst_off=dst_off):
                g.wait_ge(VEC, qv)
                src = bass.AP(wqs_b.ap().tensor, 0,
                              [[wqs_b.ap().ap[0][0], rr], [1, width]])
                dst = bass.AP(wq_ag_in.ap().tensor, dst_off + r0 * width,
                              [[width, rr], [1, width]])
                g.dma_start(out=dst, in_=src).then_inc(ST, 16)
            P.on("gpsimd", stw)
            P.inc("ST", 16)

    wslice(wg_sl, IGR, D_MODEL, 0)
    wslice(wu_sl, IGR, D_MODEL, SZ_G)
    wslice(wd_sl, DGR, I_FFN, 2 * SZ_G)
    wq_staged = P.val("ST")

    def ag_wq(g):
        g.wait_ge(ST, wq_staged)
        g.collective_compute("AllGather", ALU.bypass,
                             replica_groups=[list(range(NCORES))],
                             ins=[wq_ag_in.ap().opt()],
                             outs=[wq_ag_out.ap().opt()]).then_inc(CC, 1)
    P.on("gpsimd", ag_wq)
    cc_wq = P.inc("CC", 1)

    def load_wT(dst, piece_off, rows, rows_per_core, width):
        nrt = rows // 128
        nkt = width // 128
        for rt in range(nrt):
            g0 = rt * 128
            pieces = []
            while g0 < (rt + 1) * 128:
                c = g0 // rows_per_core
                take = min((c + 1) * rows_per_core, (rt + 1) * 128) - g0
                pieces.append((g0 - rt * 128, c, g0 - c * rows_per_core, take))
                g0 += take
            prev_t = P.val("TEN")

            def ldw(e, pieces=pieces, piece_off=piece_off, width=width, prev_t=prev_t):
                e.wait_ge(TEN, prev_t)
                for (po, c, rc, take) in pieces:
                    src = bass.AP(wq_ag_out.ap().tensor,
                                  c * WQ_ROW + piece_off + rc * width,
                                  [[width, take], [1, width]])
                    e.dma_start(out=wqs_b[po:po + take, :width],
                                in_=src).then_inc(LD2, 16)
            P.on("sync", ldw)
            ldv = P.inc("LD2", 16 * len(pieces))

            for kk in range(nkt):
                slot = [ps_t1, ps_t2][kk % 2]
                marks = P.wt_marks
                need = marks[-2] if len(marks) >= 2 else 0

                def tp(t, kk=kk, ldv=ldv, need=need, slot=slot):
                    t.wait_ge(LD2, ldv)
                    if need:
                        t.wait_ge(POOL, need)
                    t.matmul(slot[:, :], wqs_b[:, kk * 128:(kk + 1) * 128],
                             identb[:, :], is_transpose=True,
                             start=True, stop=True).then_inc(TEN, 1)
                P.on("tensor", tp)
                tv = P.inc("TEN", 1)

                def ev(sc, dst=dst, kk=kk, rt=rt, rows=rows, tv=tv, slot=slot):
                    sc.wait_ge(TEN, tv)
                    sc.copy(dst[:, kk * rows + rt * 128:
                                kk * rows + (rt + 1) * 128],
                            slot[:, :])
                    sc.drain()
                    sc.nop().then_inc(POOL, 1)
                P.on("scalar", ev)
                marks.append(P.inc("POOL", 1))

    def gate_cc(e):
        e.wait_ge(CC, cc_wq)
    P.on("sync", gate_cc)
    P.wt_marks = []
    load_wT(wgT, 0, I_FFN, IGR, D_MODEL)
    load_wT(wuT, SZ_G, I_FFN, IGR, D_MODEL)
    wT_pool = P.val("POOL")

    xbufs = [xt_a, xt_b]
    xn_tile_vec = []
    xnqT_pool = []
    ev_slot_st = {}
    P.x_marks = []
    for i in range(NTT):
        pace = xn_tile_vec[i - 2] if i >= 2 else 0

        def ldx(e, i=i, pace=pace):
            if pace:
                e.wait_ge(VEC, pace)
            e.dma_start(out=xbufs[i % 2][:, :],
                        in_=x_in[i * 128:(i + 1) * 128, :]).then_inc(LD, 16)
        P.on("scalar", ldx)
        ldv = P.inc("LD", 16)
        xb = xbufs[i % 2]
        prev_q = xn_tile_vec[-1] if xn_tile_vec else 0

        def stats(s, xb=xb, ldv=ldv, prev_q=prev_q):
            s.wait_ge(LD, ldv)
            s.wait_ge(VEC, prev_q)
            s.activation(q_t1[:, :D_MODEL], xb[:, :], AF.Copy,
                         bias=0.0, scale=1.0, accum_out=ln_a[:, :])
            s.activation(q_t2[:, :D_MODEL], xb[:, :], AF.Square,
                         bias=0.0, scale=1.0, accum_out=ln_b[:, :]).then_inc(ACTS, 1)
        P.on("scalar", stats)
        a1 = P.inc("ACTS", 1)

        def smalls(v, a1=a1):
            v.wait_ge(ACTS, a1)
            v.tensor_scalar(ln_a[:, :], ln_a[:, :], 1.0 / D_MODEL, None, ALU.mult)
            v.tensor_scalar(ln_b[:, :], ln_b[:, :], 1.0 / D_MODEL, None, ALU.mult)
            v.drain()
            v.tensor_tensor(ln_c[:, :], ln_a[:, :], ln_a[:, :], ALU.mult)
            v.drain()
            v.tensor_tensor(ln_b[:, :], ln_b[:, :], ln_c[:, :], ALU.subtract)
            v.drain()
            v.engine_nop().then_inc(VEC, 1)
        P.on("vector", smalls)
        v1 = P.inc("VEC", 1)

        def rstd(s, v1=v1):
            s.wait_ge(VEC, v1)
            s.activation(ln_c[:, :], ln_b[:, :], AF.Sqrt, bias=fconst[:, 0:1], scale=1.0).then_inc(ACTS, 1)
        P.on("scalar", rstd)
        a2 = P.inc("ACTS", 1)

        def nbias(v, a2=a2):
            v.wait_ge(ACTS, a2)
            v.reciprocal(ln_c[:, :], ln_c[:, :])
            v.drain()
            v.tensor_tensor(ln_d[:, :], ln_a[:, :], ln_c[:, :], ALU.mult)
            v.drain()
            v.tensor_scalar(ln_d[:, :], ln_d[:, :], -1.0, None, ALU.mult)
            v.drain()
            v.engine_nop().then_inc(VEC, 1)
        P.on("vector", nbias)
        v2 = P.inc("VEC", 1)

        def qtile(v, xb=xb):
            v.tensor_scalar(xn_a[:, :], xb[:, :], ln_c[:, :], ln_d[:, :],
                            ALU.mult, ALU.add)
            quant(v, xn_a, xq_a, D_MODEL)
            v.tensor_copy(xnb_a[:, :], xn_a[:, :]).then_inc(VEC, 1)
        P.on("vector", qtile)
        v3 = P.inc("VEC", 1)
        xn_tile_vec.append(v3)

        if dev:
            def devtap(g, i=i, v3=v3):
                g.wait_ge(VEC, v3)
                g.dma_start(out=dv["xn_dev"][i * 128:(i + 1) * 128, :],
                            in_=xn_a[:, :]).then_inc(ST, 16)
                g.dma_start(out=dv["xnq_dev"][i * 128:(i + 1) * 128, :],
                            in_=xq_a[:, :]).then_inc(ST, 16)
            P.on("gpsimd", devtap)
            P.inc("ST", 32)

        for kk in range(8):
            psq = ps_t1
            psn = ps_t2
            xm = P.x_marks
            prev_pool = xm[-1] if len(xm) >= 1 else 0

            def tp(t, kk=kk, v3=v3, psq=psq, psn=psn, prev_pool=prev_pool):
                t.wait_ge(VEC, v3)
                if prev_pool:
                    t.wait_ge(POOL, prev_pool)
                t.matmul(psq[:, :], xq_a[:, kk * 128:(kk + 1) * 128], identb[:, :],
                         is_transpose=True, start=True, stop=True)
                t.matmul(psn[:, :], xnb_a[:, kk * 128:(kk + 1) * 128], identb[:, :],
                         is_transpose=True, start=True, stop=True).then_inc(TEN, 1)
            P.on("tensor", tp)
            tv = P.inc("TEN", 1)
            eb = evac_a if kk % 2 == 0 else evac_b
            st_need = ev_slot_st.get(kk % 2, 0)

            def ev(sc, kk=kk, i=i, tv=tv, psq=psq, psn=psn, eb=eb, st_need=st_need):
                sc.wait_ge(TEN, tv)
                if st_need > 0:
                    sc.wait_ge(ST, st_need)
                sc.copy(xnqT[:, kk * TOK + i * 128: kk * TOK + (i + 1) * 128],
                        psq[:, :])
                sc.copy(eb[:, :], psn[:, :])
                sc.drain()
                sc.nop().then_inc(POOL, 1)
            P.on("scalar", ev)
            pv = P.inc("POOL", 1)
            P.x_marks.append(pv)

            def evd(g, kk=kk, i=i, eb=eb, pv=pv):
                g.wait_ge(POOL, pv)
                g.dma_start(out=xnT_ag_in.ap()[kk * 128:(kk + 1) * 128,
                                               i * 128:(i + 1) * 128],
                            in_=eb[:, :]).then_inc(ST, 16)
            P.on("gpsimd", evd)
            ev_slot_st[kk % 2] = P.inc("ST", 16)
        xnqT_pool.append(P.val("POOL"))

    xnT_staged = P.val("ST")

    def ag_xn(g):
        g.wait_ge(ST, xnT_staged)
        g.collective_compute("AllGather", ALU.bypass,
                             replica_groups=[list(range(NCORES))],
                             ins=[xnT_ag_in.ap().opt()],
                             outs=[xnT_ag_out.ap().opt()]).then_inc(CC, 1)
    P.on("gpsimd", ag_xn)
    cc_xn = P.inc("CC", 1)

    chunks = [(c * 512, 512) for c in range(5)] + [(2560, 256)]
    hmul_marks = []
    for i in range(NTT):
        tile_pool = xnqT_pool[i]
        for ci, (c0, cw) in enumerate(chunks):
            gi = len(hmul_marks)
            par = gi % 2
            pg = ps_a if par == 0 else ps_sp
            pu = ps_b if par == 0 else ps_big
            prev_hmul = hmul_marks[-2] if len(hmul_marks) >= 2 else 0

            def mm(t, i=i, c0=c0, cw=cw, tile_pool=tile_pool, prev_hmul=prev_hmul,
                   pg=pg, pu=pu):
                t.wait_ge(POOL, max(wT_pool, tile_pool))
                if prev_hmul:
                    t.wait_ge(VEC, prev_hmul)
                for kk in range(8):
                    t.matmul(pg[:, :cw],
                             xnqT[:, kk * TOK + i * 128: kk * TOK + (i + 1) * 128],
                             wgT[:, kk * I_FFN + c0: kk * I_FFN + c0 + cw],
                             start=(kk == 0), stop=(kk == 7))
                last = None
                for kk in range(8):
                    last = t.matmul(pu[:, :cw],
                                    xnqT[:, kk * TOK + i * 128: kk * TOK + (i + 1) * 128],
                                    wuT[:, kk * I_FFN + c0: kk * I_FFN + c0 + cw],
                                    start=(kk == 0), stop=(kk == 7))
                last.then_inc(TEN, 1)
            P.on("tensor", mm)
            tv = P.inc("TEN", 1)

            def sil(s, i=i, c0=c0, cw=cw, tv=tv, pg=pg, par=par):
                s.wait_ge(TEN, tv)
                s.activation(ffn_t1[:, par * 512:par * 512 + cw], pg[:, :cw], AF.Silu,
                             bias=0.0, scale=1.0).then_inc(ACTS, 1)
            P.on("scalar", sil)
            av = P.inc("ACTS", 1)

            def hmul(v, c0=c0, cw=cw, av=av, pu=pu, par=par):
                v.wait_ge(ACTS, av)
                v.tensor_tensor(q_u[:, c0:c0 + cw], ffn_t1[:, par * 512:par * 512 + cw],
                                pu[:, :cw], ALU.mult).then_inc(VEC, 1)
            P.on("vector", hmul)
            hmul_marks.append(P.inc("VEC", 1))

        def qh2(v):
            quant(v, q_u, hq_a, I_FFN, ub=hu_b)
            v.engine_nop().then_inc(VEC, 1)
        P.on("vector", qh2)
        vq = P.inc("VEC", 1)

        def sth(g, i=i, vq=vq):
            g.wait_ge(VEC, vq)
            g.dma_start(out=hq_scr.ap()[i * 128:(i + 1) * 128, :],
                        in_=hq_a[:, :]).then_inc(ST, 16)
            if dev:
                g.dma_start(out=dv["hq_dev"][i * 128:(i + 1) * 128, :],
                            in_=hq_a[:, :]).then_inc(ST, 16)
                if i == 0:
                    g.dma_start(out=dv["gate_dev"][:, :],
                                in_=q_t3[:, :I_FFN]).then_inc(ST, 16)
        P.on("gpsimd", sth)
        P.inc("ST", 16 + (32 if dev else 0) if False else 16)
        if dev:
            P.inc("ST", 16 + (16 if i == 0 else 0))
    hq_all = P.val("ST")
    ffn_gu_ten = P.val("TEN")

    P.wd_marks = []
    for rt in range(8):
        prev_t = P.val("TEN")

        def ldw(e, rt=rt, prev_t=prev_t):
            e.wait_ge(TEN, max(prev_t, ffn_gu_ten))
            src = bass.AP(wq_ag_out.ap().tensor, rt * WQ_ROW + 2 * SZ_G,
                          [[I_FFN, 128], [1, I_FFN]])
            e.dma_start(out=wqs_b[:, :I_FFN], in_=src).then_inc(LD2, 16)
        P.on("sync", ldw)
        ldv = P.inc("LD2", 16)
        for it in range(NIT):
            slot = [ps_t1, ps_t2][it % 2]
            marks = P.wd_marks
            need = marks[-2] if len(marks) >= 2 else 0

            def tp(t, it=it, ldv=ldv, need=need, slot=slot):
                t.wait_ge(LD2, ldv)
                if need:
                    t.wait_ge(POOL, need)
                t.matmul(slot[:, :], wqs_b[:, it * 128:(it + 1) * 128],
                         identb[:, :], is_transpose=True,
                         start=True, stop=True).then_inc(TEN, 1)
            P.on("tensor", tp)
            tv = P.inc("TEN", 1)

            def ev(sc, it=it, rt=rt, tv=tv, slot=slot):
                sc.wait_ge(TEN, tv)
                sc.copy(wdT[:, it * D_MODEL + rt * 128:
                            it * D_MODEL + (rt + 1) * 128],
                        slot[:, :])
                sc.drain()
                sc.nop().then_inc(POOL, 1)
            P.on("scalar", ev)
            marks.append(P.inc("POOL", 1))
    wd_pool = P.val("POOL")

    P.hq_marks = []
    for i in range(NTT):
        prev_ten = P.val("TEN")

        def ldh(e, i=i, prev_ten=prev_ten):
            e.wait_ge(ST, hq_all)
            e.wait_ge(TEN, prev_ten)
            e.dma_start(out=hq_in[:, :],
                        in_=hq_scr.ap()[i * 128:(i + 1) * 128, :]).then_inc(LD2, 16)
        P.on("sync", ldh)
        ldv = P.inc("LD2", 16)
        for it in range(NIT):
            slot = [ps_t1, ps_t2][it % 2]
            marks = P.hq_marks
            need = marks[-2] if len(marks) >= 2 else 0

            def tp(t, it=it, ldv=ldv, need=need, slot=slot):
                t.wait_ge(LD2, ldv)
                if need:
                    t.wait_ge(POOL, need)
                t.matmul(slot[:, :], hq_in[:, it * 128:(it + 1) * 128],
                         identb[:, :], is_transpose=True,
                         start=True, stop=True).then_inc(TEN, 1)
            P.on("tensor", tp)
            tv = P.inc("TEN", 1)

            def ev(sc, it=it, tv=tv, slot=slot):
                sc.wait_ge(TEN, tv)
                sc.copy(hqT_b[:, it * 128:(it + 1) * 128],
                        slot[:, :])
                sc.drain()
                sc.nop().then_inc(POOL, 1)
            P.on("scalar", ev)
            marks.append(P.inc("POOL", 1))
        hT_pool = P.val("POOL")
        prev_gsil = P.val("ACTS")

        def dmm(t, hT_pool=hT_pool, prev_gsil=prev_gsil):
            t.wait_ge(POOL, max(hT_pool, wd_pool))
            t.wait_ge(ACTS, prev_gsil)
            last = None
            for half in range(2):
                for it in range(NIT):
                    last = t.matmul(ps_big[:, half * 512:(half + 1) * 512],
                                    hqT_b[:, it * 128:(it + 1) * 128],
                                    wdT[:, it * D_MODEL + half * 512:
                                        it * D_MODEL + (half + 1) * 512],
                                    start=(it == 0), stop=(it == NIT - 1))
            last.then_inc(TEN, 1)
        P.on("tensor", dmm)
        tv = P.inc("TEN", 1)

        def gsil(s, tv=tv):
            s.wait_ge(TEN, tv)
            s.activation(gout_b[:, :], ps_big[:, :], AF.Silu,
                         bias=0.0, scale=1.0).then_inc(ACTS, 1)
        P.on("scalar", gsil)
        av = P.inc("ACTS", 1)

        def gst(g, i=i, av=av):
            g.wait_ge(ACTS, av)
            g.dma_start(out=g_scr.ap()[i * 128:(i + 1) * 128, :],
                        in_=gout_b[:, :]).then_inc(ST, 16)
            if dev:
                g.dma_start(out=dv["g_dev"][i * 128:(i + 1) * 128, :],
                            in_=gout_b[:, :]).then_inc(ST, 16)
        P.on("gpsimd", gst)
        P.inc("ST", 16 + (16 if dev else 0))
    g_all = P.val("ST")


    vec_ffn = P.val("VEC")
    ten_ffn = P.val("TEN")
    act_ffn = P.val("ACTS")
    if stage in ("pass1", "scan", "mid", "full"):
        STB = 256
        NSB = L // STB
        MB = xt_a_off

        def msb(name, shape, dt, off):
            return sb(name, shape, dt, at=MB + off)

        m_rhs = msb("m_rhs", [128, 2 * 8 * 512], BF16, 0)
        m_winT = msb("m_winT", [128, 32 * 128], BF16, 16384)
        m_wst = msb("m_wst", [128, 512], F32, 24576)
        m_xs = msb("m_xs", [128, 4 * 520], F32, 26624)
        m_xc = msb("m_xc", [128, 2 * 512], F32, 35328)
        m_xcb = msb("m_xcb", [128, 2 * 512], BF16, 39424)
        m_zst = msb("m_zst", [128, 2 * 512], F32, 41472)
        m_xpst = msb("m_xpst", [33, 512], F32, 45568)
        m_xpw = msb("m_xpw", [128, 2 * 33], BF16, 47616)
        m_xpwf = msb("m_xpwf", [128, 2 * 33], F32, 47872)
        SCB = 49152
        s_bc = msb("s_bc", [128, 2 * 33 * STB], F32, SCB)
        s_xc2 = msb("s_xc2", [128, 4 * STB], F32, SCB + 67584)
        s_z2 = msb("s_z2", [128, 4 * STB], F32, SCB + 71680)
        s_dt = msb("s_dt", [128, 2 * STB], F32, SCB + 75776)
        s_U = msb("s_U", [128, 2 * STB], F32, SCB + 77824)
        s_dA = msb("s_dA", [128, 16 * STB], F32, SCB + 79872)
        s_dBx = msb("s_dBx", [128, 16 * STB], F32, SCB + 96256)
        s_h = msb("s_h", [128, 16 * STB], F32, SCB + 112640)
        s_hprev = msb("s_hprev", [128, 2 * 16], F32, SCB + 129024)
        s_A = msb("s_A", [128, 32], F32, SCB + 129280)
        s_Af = msb("s_Af", [128, 32], F32, SCB + 129536)
        s_zs = msb("s_zs", [128, 256], F32, SCB + 129792)
        s_yz = msb("s_yz", [128, 256], F32, SCB + 130816)
        s_yzT = msb("s_yzT", [128, 2 * STB], BF16, SCB + 131840)
        s_tmp = msb("s_tmp", [128, 512], F32, SCB + 132864)
        s_hc = msb("s_hc", [128, 16 * STB], F32, SCB + 134976)
        s_xcd = msb("s_xcd", [128, 4 * STB], F32, 0)
        OB = SCB
        o_wout = msb("o_wout", [128, 16 * 1024], BF16, OB)
        o_yzT = msb("o_yzT", [128, 16 * TOK], BF16, OB + 32768)
        o_x = msb("o_x", [128, D_MODEL], F32, OB + 32768 + 65536)
        o_g = msb("o_g", [128, D_MODEL], F32, OB + 32768 + 69632)
        o_t1 = msb("o_t1", [128, D_MODEL], F32, OB + 32768 + 73728)

        def mconst(e):
            e.wait_ge(ST, g_all)
            e.dma_start(out=m_xpwf[:, 0:33], in_=xpw_sl[0:128, :]).then_inc(LD, 16)
            e.dma_start(out=m_xpwf[:, 33:66], in_=xpw_sl[128:256, :]).then_inc(LD, 16)
            e.dma_start(out=s_Af[:, 0:16], in_=alog_sl[0:128, :]).then_inc(LD, 16)
            e.dma_start(out=s_Af[:, 16:32], in_=alog_sl[128:256, :]).then_inc(LD, 16)
            e.dma_start(out=s_tmp[:, 0:4], in_=conv_sl[0:128, :]).then_inc(LD, 16)
            e.dma_start(out=s_tmp[:, 4:8], in_=conv_sl[128:256, :]).then_inc(LD, 16)
            e.dma_start(out=s_tmp[:, 8:10], in_=convb_sl).then_inc(LD, 16)
            e.dma_start(out=s_tmp[:, 10:12], in_=d_sl).then_inc(LD, 16)
            e.dma_start(out=s_tmp[:, 12:14], in_=dtw_sl).then_inc(LD, 16)
            e.dma_start(out=s_tmp[:, 14:16], in_=dtb_sl).then_inc(LD, 16)
        P.on("sync", mconst)
        ld_mc = P.inc("LD", 160)
        m_cols = msb("m_cols", [128, 16], F32, SCB + 134912)
        def mccp(v):
            v.wait_ge(LD, ld_mc)
            v.tensor_copy(m_cols[:, :], s_tmp[:, 0:16])
            v.tensor_copy(m_xpw[:, 0:66], m_xpwf[:, 0:66])
            v.drain()
            v.engine_nop().then_inc(VEC, 1)
        P.on("vector", mccp)
        v_mc = P.inc("VEC", 1)

        def mA(sc):
            sc.wait_ge(LD, ld_mc)
            sc.activation(s_A[:, :], s_Af[:, :], AF.Exp, bias=0.0, scale=1.0).then_inc(ACTS, 1)
        P.on("scalar", mA)
        a_mA = P.inc("ACTS", 1)
        def mA2(v):
            v.wait_ge(ACTS, a_mA)
            v.tensor_scalar(s_A[:, :], s_A[:, :], -1.0, None, ALU.mult)
            v.drain()
            v.engine_nop().then_inc(VEC, 1)
        P.on("vector", mA2)
        v_mA = P.inc("VEC", 1)

        for kk in range(8):
            pace_v = P.val("VEC")

            def lw(e, kk=kk, pace_v=pace_v):
                e.wait_ge(VEC, pace_v)
                e.dma_start(out=m_wst[:, :], in_=w_inT_sl[kk * 128:(kk + 1) * 128, :]).then_inc(LD, 16)
            P.on("sync", lw)
            ldv = P.inc("LD", 16)
            def cw(v, kk=kk, ldv=ldv):
                v.wait_ge(LD, ldv)
                v.tensor_copy(m_winT[:, kk * 512:(kk + 1) * 512], m_wst[:, :])
                v.drain()
                v.engine_nop().then_inc(VEC, 1)
            P.on("vector", cw)
            P.inc("VEC", 1)
        v_wl = P.val("VEC")

        NTB = L // 512
        P.xpev_marks = {}
        P.evac_marks = {}
        if p1n is None:
            p1n = NTB
        conv_carry_vec = 0
        xp_st = 0
        for tb in range(p1n):
            c = tb // 4
            col0 = (tb % 4) * 512
            buf = tb % 2
            prev_ten = P.val("TEN")

            def ldr(e, c=c, col0=col0, buf=buf, tb=tb, prev_ten=prev_ten):
                if tb == 0:
                    e.wait_ge(CC, cc_xn)
                    e.wait_ge(VEC, v_wl)
                if tb >= 2:
                    e.wait_ge(TEN, prev_ten)
                for kk in range(8):
                    if "tbld" in skip:
                        e.dma_start(out=m_rhs[0:1, 0:8],
                                    in_=xnT_ag_out.ap()[0:1, 0:8]).then_inc(LD2, 16)
                        continue
                    e.dma_start(out=m_rhs[:, buf * 4096 + kk * 512: buf * 4096 + (kk + 1) * 512],
                                in_=bass.AP(xnT_ag_out.ap().tensor,
                                            (c * D_MODEL + kk * 128) * TOK + col0,
                                            [[TOK, 128], [1, 512]])).then_inc(LD2, 16)
            P.on("sync", ldr)
            ldv = P.inc("LD2", 128)

            psums = [ps_a[:, :], ps_b[:, :], ps_big[:, 0:512], ps_big[:, 512:1024]]
            prev_act = P.evac_marks.get(tb - 1, 0)
            prev_vec = P.val("VEC")

            def mmip(t, ldv=ldv, buf=buf, prev_act=prev_act, prev_vec=prev_vec, tb=tb):
                t.wait_ge(LD2, ldv)
                if tb > 0:
                    t.wait_ge(ACTS, prev_act)
                    t.wait_ge(VEC, prev_vec)
                last = None
                if "tbmm" not in skip:
                    for mi in range(4):
                        for kk in range(8):
                            last = t.matmul(psums[mi],
                                            m_winT[:, kk * 512 + mi * 128: kk * 512 + (mi + 1) * 128],
                                            m_rhs[:, buf * 4096 + kk * 512: buf * 4096 + (kk + 1) * 512],
                                            start=(kk == 0), stop=(kk == 7))
                (last if last is not None else t.nop()).then_inc(TEN, 1)
            P.on("tensor", mmip)
            tv_ip = P.inc("TEN", 1)

            def zev(sc, tv_ip=tv_ip):
                sc.wait_ge(TEN, tv_ip)
                sc.copy(m_zst[:, 0:512], ps_big[:, 0:512])
                sc.copy(m_zst[:, 512:1024], ps_big[:, 512:1024])
                sc.drain()
                sc.nop().then_inc(ACTS, 1)
            P.on("scalar", zev)
            a_z = P.inc("ACTS", 1)

            def zst(g, tb=tb, a_z=a_z):
                g.wait_ge(ACTS, a_z)
                if "tbz" in skip:
                    g.dma_start(out=z_scr.ap()[0:1, 0:8], in_=m_zst[0:1, 0:8]).then_inc(ST, 16)
                    g.dma_start(out=z_scr.ap()[0:1, 0:8], in_=m_zst[0:1, 0:8]).then_inc(ST, 16)
                    return
                g.dma_start(out=bass.AP(z_scr.ap().tensor, 0 * L + tb * 512,
                                        [[L, 128], [1, 512]]),
                            in_=m_zst[:, 0:512]).then_inc(ST, 16)
                g.dma_start(out=bass.AP(z_scr.ap().tensor, 128 * L + tb * 512,
                                        [[L, 128], [1, 512]]),
                            in_=m_zst[:, 512:1024]).then_inc(ST, 16)
            P.on("gpsimd", zst)
            P.inc("ST", 32)

            for dg in range(2):
                xs_cur = m_xs[:, (buf * 2 + dg) * 520: (buf * 2 + dg) * 520 + 515]
                xs_prv = m_xs[:, ((1 - buf) * 2 + dg) * 520: ((1 - buf) * 2 + dg) * 520 + 515]

                def cvev(sc, dg=dg, tv_ip=tv_ip, xs_cur=xs_cur, xs_prv=xs_prv, tb=tb,
                         prev_vec=prev_vec):
                    sc.wait_ge(TEN, tv_ip)
                    if tb > 0:
                        sc.wait_ge(VEC, prev_vec)
                    sc.copy(xs_cur[:, 3:515], psums[dg])
                    if tb == 0:
                        pass
                    sc.drain()
                    sc.nop().then_inc(ACTS, 1)
                P.on("scalar", cvev)
                a_cv = P.inc("ACTS", 1)
                P.cvev_last = a_cv

                def conv(v, dg=dg, a_cv=a_cv, xs_cur=xs_cur, xs_prv=xs_prv, tb=tb):
                    v.wait_ge(ACTS, a_cv)
                    if "tbcv" in skip:
                        v.engine_nop().then_inc(VEC, 1)
                        return
                    if tb == 0:
                        v.memset(xs_cur[:, 0:3], 0.0)
                    else:
                        v.tensor_copy(xs_cur[:, 0:3], xs_prv[:, 512:515])
                    v.drain()
                    v.tensor_scalar(s_tmp[:, 0:512], xs_cur[:, 0:512],
                                    m_cols[:, dg * 4:dg * 4 + 1], None, ALU.mult)
                    v.scalar_tensor_tensor(s_tmp[:, 0:512], xs_cur[:, 1:513],
                                           m_cols[:, dg * 4 + 1:dg * 4 + 2], s_tmp[:, 0:512],
                                           ALU.mult, ALU.add)
                    v.scalar_tensor_tensor(s_tmp[:, 0:512], xs_cur[:, 2:514],
                                           m_cols[:, dg * 4 + 2:dg * 4 + 3], s_tmp[:, 0:512],
                                           ALU.mult, ALU.add)
                    v.scalar_tensor_tensor(s_tmp[:, 0:512], xs_cur[:, 3:515],
                                           m_cols[:, dg * 4 + 3:dg * 4 + 4], s_tmp[:, 0:512],
                                           ALU.mult, ALU.add)
                    v.drain()
                    v.engine_nop().then_inc(VEC, 1)
                P.on("vector", conv)
                v_cv = P.inc("VEC", 1)

                def csil(sc, dg=dg, v_cv=v_cv):
                    sc.wait_ge(VEC, v_cv)
                    sc.activation(m_xc[:, dg * 512:(dg + 1) * 512], s_tmp[:, 0:512],
                                  AF.Silu, bias=m_cols[:, 8 + dg:9 + dg], scale=1.0).then_inc(ACTS, 1)
                P.on("scalar", csil)
                a_cs = P.inc("ACTS", 1)

                def cstore(g, dg=dg, tb=tb, a_cs=a_cs):
                    g.wait_ge(ACTS, a_cs)
                    if "tbc" in skip:
                        g.dma_start(out=xc_scr.ap()[0:1, 0:8], in_=m_xc[0:1, 0:8]).then_inc(ST, 16)
                        return
                    g.dma_start(out=bass.AP(xc_scr.ap().tensor, dg * 128 * L + tb * 512,
                                            [[L, 128], [1, 512]]),
                                in_=m_xc[:, dg * 512:(dg + 1) * 512]).then_inc(ST, 16)
                P.on("gpsimd", cstore)
                P.inc("ST", 16)

                def cbf(v, dg=dg, a_cs=a_cs):
                    v.wait_ge(ACTS, a_cs)
                    v.tensor_copy(m_xcb[:, dg * 512:(dg + 1) * 512],
                                  m_xc[:, dg * 512:(dg + 1) * 512])
                    v.drain()
                    v.engine_nop().then_inc(VEC, 1)
                P.on("vector", cbf)
                P.inc("VEC", 1)
            v_cbf = P.val("VEC")
            P.evac_marks[tb] = P.cvev_last

            def mmxp(t, v_cbf=v_cbf, tb=tb):
                t.wait_ge(VEC, v_cbf)
                if tb > 0:
                    t.wait_ge(ACTS, P.xpev_marks[tb - 1])
                if "tbxp" in skip:
                    t.nop().then_inc(TEN, 1)
                    return
                t.matmul(ps_xp[:, :], m_xpw[:, 0:33], m_xcb[:, 0:512],
                         start=True, stop=False)
                t.matmul(ps_xp[:, :], m_xpw[:, 33:66], m_xcb[:, 512:1024],
                         start=False, stop=True).then_inc(TEN, 1)
            P.on("tensor", mmxp)
            tv_xp = P.inc("TEN", 1)

            def xpev(sc, tv_xp=tv_xp):
                sc.wait_ge(TEN, tv_xp)
                sc.copy(m_xpst[:, :], ps_xp[:, :])
                sc.drain()
                sc.nop().then_inc(ACTS, 1)
            P.on("scalar", xpev)
            a_xp = P.inc("ACTS", 1)
            P.xpev_marks[tb] = a_xp

            def xpst(g, tb=tb, a_xp=a_xp):
                g.wait_ge(ACTS, a_xp)
                if "tbxs" in skip:
                    g.dma_start(out=xp_ar_in.ap()[0:1, 0:8], in_=m_xpst[0:1, 0:8]).then_inc(ST, 16)
                    return
                g.dma_start(out=bass.AP(xp_ar_in.ap().tensor, tb * 512,
                                        [[L, 33], [1, 512]]),
                            in_=m_xpst[:, :]).then_inc(ST, 16)
            P.on("gpsimd", xpst)
            xp_st = P.inc("ST", 16)

        def ar_xp(g, xp_st=xp_st):
            g.wait_ge(ST, xp_st)
            if "ar" in skip:
                g.nop().then_inc(CC, 1)
                return
            g.collective_compute("AllReduce", ALU.add,
                                 replica_groups=[list(range(NCORES))],
                                 ins=[xp_ar_in.ap().opt()],
                                 outs=[xp_ar_out.ap().opt()]).then_inc(CC, 1)
        P.on("gpsimd", ar_xp)
        cc_ar = P.inc("CC", 1)

        ld_dtr = P.val("LD")

        if dev:
            dv["xp_dev"] = nc.dram_tensor("xp_dev", [33, L], F32, kind="ExternalOutput").ap()
            dv["xc_dev"] = nc.dram_tensor("xc_dev", [DI, L], F32, kind="ExternalOutput").ap()
            dv["z_dev"] = nc.dram_tensor("z_dev", [DI, L], F32, kind="ExternalOutput").ap()

            def p1tap(g):
                g.wait_ge(CC, cc_ar)
                g.dma_start(out=dv["xp_dev"], in_=xp_ar_out.ap()[:, :]).then_inc(ST, 16)
                g.dma_start(out=dv["xc_dev"], in_=xc_scr.ap()[:, :]).then_inc(ST, 16)
                g.dma_start(out=dv["z_dev"], in_=z_scr.ap()[:, :]).then_inc(ST, 16)
            P.on("gpsimd", p1tap)
            P.inc("ST", 48)
        if stage == "pass1":
            fin0 = P.val("ST")

            def fw0(g):
                g.wait_ge(ST, fin0)
                g.dma_start(out=out[0:128, :], in_=gout_b[:, :]).then_inc(ST, 16)
                g.wait_ge(ST, fin0 + 16)
            P.on("gpsimd", fw0)
            P.inc("ST", 16)

        if dev:
            dv["scan_dev"] = nc.dram_tensor("scan_dev", [128, 16384], F32, kind="ExternalOutput").ap()

        ag_yz_st = 0
        P.vec_done = {}
        P.ten_done = {}
        P.v_scn_last = 0
        P.a_dA_last = 0
        P.pending_extract = None
        for blk in (range(NSB) if stage != "pass1" else []):
            bbuf = blk % 2
            bco = bbuf * 33 * STB
            pv2 = P.vec_done.get(blk - 2, 0)
            pt2 = P.ten_done.get(blk - 2, 0)

            def bcload(e, blk=blk, bco=bco, pv2=pv2, pt2=pt2, bbuf=bbuf):
                e.wait_ge(CC, cc_ar)
                if blk >= 2:
                    e.wait_ge(VEC, pv2)
                    e.wait_ge(TEN, pt2)
                for dg in range(2):
                    e.dma_start(out=s_xc2[:, (bbuf * 2 + dg) * STB:(bbuf * 2 + dg + 1) * STB],
                                in_=bass.AP(xc_scr.ap().tensor, dg * 128 * L + blk * STB,
                                            [[L, 128], [1, STB]])).then_inc(LD3, 16)
                    e.dma_start(out=s_z2[:, (bbuf * 2 + dg) * STB:(bbuf * 2 + dg + 1) * STB],
                                in_=bass.AP(z_scr.ap().tensor, dg * 128 * L + blk * STB,
                                            [[L, 128], [1, STB]])).then_inc(LD3, 16)
                for p in range(16):
                    e.dma_start(out=s_bc[p:p + 1, bco:bco + 33 * STB],
                                in_=bass.AP(xp_ar_out.ap().tensor, blk * STB,
                                            [[0, 1], [L, 33], [1, STB]])).then_inc(LD3, 16)
                n = 16
                base = P.bcv[blk]
                step = 0
                while n < 128:
                    m = min(n, 128 - n)
                    e.wait_ge(LD3, base + 64 + 256 + step * 16)
                    e.dma_start(out=s_bc[n:n + m, bco:bco + 33 * STB],
                                in_=s_bc[0:m, bco:bco + 33 * STB]).then_inc(LD3, 16)
                    step += 1
                    n += m
            if not hasattr(P, "bcv"):
                P.bcv = {}
            P.bcv[blk] = P.val("LD3")
            P.on("sync", bcload)
            P.inc("LD3", 64 + 16 * 16 + 16 * 3)
            ld_blk = P.val("LD3")

            prev = P.val("ACTS")

            ten_prev_blk = P.val("TEN")

            if dev and blk == 0:
                def earlytap(g, ld_blk=ld_blk):
                    g.wait_ge(LD3, ld_blk)
                    g.dma_start(out=dv["scan_dev"][:, 6656:6912],
                                in_=s_bc[:, 32 * STB:33 * STB]).then_inc(ST, 16)
                    g.dma_start(out=dv["scan_dev"][:, 7168:7680],
                                in_=s_xc2[:, 0:512]).then_inc(ST, 16)
                P.on("gpsimd", earlytap)
                et_st = P.inc("ST", 32)

                def holdtap(v, et_st=et_st):
                    v.wait_ge(ST, et_st)
                P.on("vector", holdtap)

            tapst = getattr(P, "scantap_st", {}).get(blk - 1, 0) if dev else 0

            def dtc1(v, ld_blk=ld_blk, bco=bco, pt2=pt2, tapst=tapst,
                     a_prev=P.a_dA_last):
                v.wait_ge(LD3, ld_blk)
                if a_prev:
                    v.wait_ge(ACTS, a_prev)
                v.wait_ge(TEN, pt2)
                if tapst:
                    v.wait_ge(ST, tapst)
                for dg in range(2):
                    v.tensor_scalar(s_dt[:, dg * STB:(dg + 1) * STB],
                                    s_bc[:, bco:bco + STB],
                                    m_cols[:, 12 + dg:13 + dg],
                                    m_cols[:, 14 + dg:15 + dg],
                                    ALU.mult, ALU.add)
                v.drain()
                v.tensor_scalar(s_tmp[:, 0:512].bitcast(I32), s_dt[:, 0:512].bitcast(I32),
                                maskc[:, 0:1], None, ALU.bitwise_and)
                v.engine_nop().then_inc(VEC, 1)
            P.on("vector", dtc1)
            v_d1 = P.inc("VEC", 1)

            def dtc2(sc, v_d1=v_d1):
                sc.wait_ge(VEC, v_d1)
                sc.activation(s_tmp[:, 0:512], s_tmp[:, 0:512], AF.Exp, bias=0.0, scale=-1.0)
                sc.activation(s_tmp[:, 0:512], s_tmp[:, 0:512], AF.Ln,
                              bias=fconst[:, 1:2], scale=1.0)
                sc.activation(s_U[:, 0:512], s_dt[:, 0:512], AF.Relu, bias=0.0, scale=1.0)
                sc.drain()
                sc.nop().then_inc(ACTS, 1)
            P.on("scalar", dtc2)
            a_d2 = P.inc("ACTS", 1)

            def dtc3(v, a_d2=a_d2, bbuf=bbuf):
                v.wait_ge(ACTS, a_d2)
                v.tensor_tensor(s_dt[:, 0:512], s_U[:, 0:512], s_tmp[:, 0:512], ALU.add)
                v.tensor_tensor(s_U[:, 0:512], s_dt[:, 0:512],
                                s_xc2[:, bbuf * 512:(bbuf + 1) * 512], ALU.mult)
                v.engine_nop().then_inc(VEC, 1)
            P.on("vector", dtc3)
            v_d3 = P.inc("VEC", 1)

            for dg in range(2):
                ten_tpy_prev = P.val("TEN")

                def dAe(sc, dg=dg, v_d3=v_d3, prev_scn=P.v_scn_last):
                    sc.wait_ge(VEC, v_d3)
                    if prev_scn:
                        sc.wait_ge(VEC, prev_scn)
                    for n in range(N_ST):
                        ins = sc.activation(s_dA[:, n * STB:(n + 1) * STB],
                                            s_dt[:, dg * STB:(dg + 1) * STB], AF.Exp,
                                            bias=0.0, scale=s_A[:, dg * 16 + n:dg * 16 + n + 1])
                        if n in (4, 8, 12):
                            ins.then_inc(ACTS, 1)
                    sc.nop().then_inc(ACTS, 1)
                P.on("scalar", dAe)
                P.inc("ACTS", 3)
                a_dA = P.inc("ACTS", 1)
                P.a_dA_last = a_dA

                def scn(v, dg=dg, blk=blk, a_dA=a_dA, bco=bco, bbuf=bbuf,
                        ten_tpy_prev=ten_tpy_prev, pend=P.pending_extract):
                    prow = s_U.ap().ap[0][0]
                    ub = bass.AP(s_U.ap().tensor, dg * STB, [[prow, 128], [0, 16], [1, STB]])
                    bv = bass.AP(s_bc.ap().tensor, bco + STB, [[s_bc.ap().ap[0][0], 128], [STB, 16], [1, STB]])
                    dxv = bass.AP(s_dBx.ap().tensor, 0, [[s_dBx.ap().ap[0][0], 128], [STB, 16], [1, STB]])
                    v.tensor_tensor(dxv, ub, bv, ALU.mult)
                    if pend is not None:
                        hl = bass.AP(s_h.ap().tensor, STB - 1, [[s_h.ap().ap[0][0], 128], [STB, 16]])
                        v.tensor_copy(s_hprev[:, pend * 16:(pend + 1) * 16], hl)
                    for n in range(N_ST):
                        if n % 4 == 0:
                            v.wait_ge(ACTS, a_dA - 3 + n // 4)
                        init = 0.0 if blk == 0 else s_hprev[:, dg * 16 + n:dg * 16 + n + 1]
                        v.tensor_tensor_scan(s_h[:, n * STB:(n + 1) * STB],
                                             s_dA[:, n * STB:(n + 1) * STB],
                                             s_dBx[:, n * STB:(n + 1) * STB],
                                             init, ALU.mult, ALU.add)
                    v.engine_nop().then_inc(VEC, 1)
                    v.wait_ge(TEN, ten_tpy_prev)
                    cv = bass.AP(s_bc.ap().tensor, bco + 17 * STB,
                                 [[s_bc.ap().ap[0][0], 128], [STB, 16], [1, STB]])
                    hv = bass.AP(s_h.ap().tensor, 0, [[s_h.ap().ap[0][0], 128], [STB, 16], [1, STB]])
                    hcv = bass.AP(s_hc.ap().tensor, 0, [[s_hc.ap().ap[0][0], 128], [STB, 16], [1, STB]])
                    v.tensor_tensor(hcv, hv, cv, ALU.mult)
                    v.tensor_scalar(s_xcd[:, (bbuf * 2 + dg) * STB:(bbuf * 2 + dg + 1) * STB],
                                    s_xc2[:, (bbuf * 2 + dg) * STB:(bbuf * 2 + dg + 1) * STB],
                                    m_cols[:, 10 + dg:11 + dg], None, ALU.mult)
                    v.engine_nop().then_inc(VEC, 1)
                P.on("vector", scn)
                P.inc("VEC", 1)
                v_sc = P.inc("VEC", 1)
                P.v_scn_last = v_sc - 1
                P.pending_extract = dg

                if dev and blk < 2 and dg == 0:
                    P.scantap_st = getattr(P, "scantap_st", {})

                    def scantap(g, blk=blk, v_sc=v_sc):
                        g.wait_ge(VEC, v_sc)
                        g.dma_start(out=dv["scan_dev"][:, blk * 8192:blk * 8192 + 512],
                                    in_=s_dt[:, 0:512]).then_inc(ST, 16)
                        g.dma_start(out=dv["scan_dev"][:, blk * 8192 + 512:blk * 8192 + 4608],
                                    in_=s_h[:, :]).then_inc(ST, 16)
                        g.dma_start(out=dv["scan_dev"][:, blk * 8192 + 4608:blk * 8192 + 4608 + 512],
                                    in_=s_U[:, 0:512]).then_inc(ST, 16)
                        g.dma_start(out=dv["scan_dev"][:, blk * 8192 + 5120:blk * 8192 + 5120 + 512],
                                    in_=s_bc[:, (blk % 2) * 32 * STB:(blk % 2) * 32 * STB + 512]).then_inc(ST, 16)
                        g.dma_start(out=dv["scan_dev"][:, blk * 8192 + 5632:blk * 8192 + 5632 + 256],
                                    in_=s_bc[:, (blk % 2) * 33 * STB + 32 * STB:
                                             (blk % 2) * 33 * STB + 33 * STB]).then_inc(ST, 16)
                        g.dma_start(out=dv["scan_dev"][:, blk * 8192 + 6144:blk * 8192 + 6144 + 512],
                                    in_=s_bc[:, 0:512]).then_inc(ST, 16)
                    P.on("gpsimd", scantap)
                    P.scantap_st[blk] = P.inc("ST", 96)

                for ch in range(STB // 128):
                    def tpy(t, dg=dg, ch=ch, v_sc=v_sc, blk=blk):
                        t.wait_ge(VEC, v_sc)
                        last = None
                        for n in range(N_ST):
                            last = t.matmul(ps_a[:, ch * 256 + dg * 128: ch * 256 + (dg + 1) * 128],
                                            s_hc[:, n * STB + ch * 128: n * STB + (ch + 1) * 128],
                                            ident[:, :], is_transpose=True,
                                            start=(dg == 0 and ch == 0 and n == 0),
                                            stop=(n == N_ST - 1), skip_group_check=True)
                        last.then_inc(TEN, 1)
                    P.on("tensor", tpy)
                    P.inc("TEN", 1)
            v_g1 = P.val("VEC")

            for ch in range(STB // 128):
                prev_bev = P.val("ACTS")

                def tpxz(t, ch=ch, v_g1=v_g1, prev_bev=prev_bev, bbuf=bbuf):
                    t.wait_ge(VEC, v_g1)
                    t.wait_ge(ACTS, prev_bev)
                    last = None
                    for dg in range(2):
                        t.matmul(ps_a[:, ch * 256 + dg * 128: ch * 256 + (dg + 1) * 128],
                                 s_xcd[:, (bbuf * 2 + dg) * STB + ch * 128:
                                       (bbuf * 2 + dg) * STB + (ch + 1) * 128],
                                 ident[:, :], is_transpose=True, start=False, stop=True,
                                 skip_group_check=True)
                        last = t.matmul(ps_big[:, ch * 256 + dg * 128: ch * 256 + (dg + 1) * 128],
                                        s_z2[:, (bbuf * 2 + dg) * STB + ch * 128:
                                             (bbuf * 2 + dg) * STB + (ch + 1) * 128],
                                        ident[:, :], is_transpose=True,
                                        start=(ch == 0 and dg == 0), stop=True,
                                        skip_group_check=True)
                    last.then_inc(TEN, 1)
                P.on("tensor", tpxz)
                tv_xz = P.inc("TEN", 1)

                def gsl(sc, ch=ch, tv_xz=tv_xz):
                    sc.wait_ge(TEN, tv_xz)
                    sc.activation(s_zs[:, :], ps_big[:, ch * 256:ch * 256 + 256], AF.Silu,
                                  bias=0.0, scale=1.0)
                    sc.drain()
                    sc.nop().then_inc(ACTS, 1)
                P.on("scalar", gsl)
                a_gs = P.inc("ACTS", 1)

                def gat2(v, ch=ch, a_gs=a_gs):
                    v.wait_ge(ACTS, a_gs)
                    v.tensor_tensor(s_yz[:, :], ps_a[:, ch * 256:ch * 256 + 256],
                                    s_zs[:, :], ALU.mult)
                    v.drain()
                    v.engine_nop().then_inc(VEC, 1)
                P.on("vector", gat2)
                v_g2 = P.inc("VEC", 1)

                def tpb(t, ch=ch, v_g2=v_g2):
                    t.wait_ge(VEC, v_g2)
                    last = None
                    for dg in range(2):
                        last = t.matmul(ps_big[:, 512 + ch * 256 + dg * 128:
                                               512 + ch * 256 + (dg + 1) * 128],
                                        s_yz[:, dg * 128:(dg + 1) * 128],
                                        ident[:, :], is_transpose=True,
                                        start=(ch == 0 and dg == 0), stop=True,
                                        skip_group_check=True)
                    last.then_inc(TEN, 1)
                P.on("tensor", tpb)
                tv_b = P.inc("TEN", 1)

                def bev(sc, ch=ch, tv_b=tv_b):
                    sc.wait_ge(TEN, tv_b)
                    sc.copy(s_yzT[:, 0 * STB + ch * 128: 0 * STB + (ch + 1) * 128],
                            ps_big[:, 512 + ch * 256:512 + ch * 256 + 128])
                    sc.copy(s_yzT[:, 1 * STB + ch * 128: 1 * STB + (ch + 1) * 128],
                            ps_big[:, 512 + ch * 256 + 128:512 + ch * 256 + 256])
                    sc.drain()
                    sc.nop().then_inc(ACTS, 1)
                P.on("scalar", bev)
                P.inc("ACTS", 1)
            a_bev = P.val("ACTS")

            if dev and blk == 0:
                def yztap(g, a_bev=a_bev):
                    g.wait_ge(ACTS, a_bev)
                    g.dma_start(out=dv["scan_dev"][:, 7680:7936],
                                in_=s_yz[:, 0:256]).then_inc(ST, 16)
                    g.dma_start(out=dv["scan_dev"][:, 7936:8192],
                                in_=s_zs[:, 0:256]).then_inc(ST, 16)
                    g.dma_start(out=dv["scan_dev"][:, 14848:15360],
                                in_=s_yzT.ap().bitcast(F32)[:, 0:512] if False else s_tmp[:, 0:512]).then_inc(ST, 16)
                P.on("gpsimd", yztap)
                P.scantap_st[0] = P.inc("ST", 48)

            def yst(g, blk=blk, a_bev=a_bev):
                g.wait_ge(ACTS, a_bev)
                for dg in range(2):
                    g.dma_start(out=bass.AP(yz_ag_in.ap().tensor, dg * 128 * L + blk * STB,
                                            [[L, 128], [1, STB]]),
                                in_=s_yzT[:, dg * STB:(dg + 1) * STB]).then_inc(ST, 16)
            P.on("gpsimd", yst)
            ag_yz_st = P.inc("ST", 32)
            P.vec_done[blk] = P.val("VEC")
            P.ten_done[blk] = P.val("TEN")

        if dev and stage in ("scan", "mid", "full"):
            dv["yzT_dev"] = nc.dram_tensor("yzT_dev", [DI, L], BF16, kind="ExternalOutput").ap()
            lastst = P.val("ST")

            def p2tap(g, lastst=lastst):
                g.wait_ge(ST, lastst)
                g.dma_start(out=dv["yzT_dev"], in_=yz_ag_in.ap()[:, :]).then_inc(ST, 16)
            P.on("gpsimd", p2tap)
            P.inc("ST", 16)

        if stage in ("pass1", "scan"):
            finp = P.val("ST")

            def fwp(g, finp=finp):
                g.wait_ge(ST, finp)
                g.dma_start(out=out[0:128, :], in_=gout_b[:, :]).then_inc(ST, 16)
                g.wait_ge(ST, finp + 16)
            P.on("gpsimd", fwp)
            P.inc("ST", 16)

        do_out = stage in ("mid", "full")
        if do_out:
            def ag_yz(g, ag_yz_st=ag_yz_st):
                g.wait_ge(ST, ag_yz_st)
                g.collective_compute("AllGather", ALU.bypass,
                                     replica_groups=[list(range(NCORES))],
                                     ins=[yz_ag_in.ap().opt()],
                                     outs=[yz_ag_out.ap().opt()]).then_inc(CC, 1)
            P.on("gpsimd", ag_yz)
            cc_yz = P.inc("CC", 1)

        vec_scan_end = P.val("VEC")
        pid_tok0 = []
        for di in (range(16) if do_out else []):
            def lwo(e, di=di, vec_scan_end=vec_scan_end):
                if di == 0:
                    e.wait_ge(VEC, vec_scan_end)
                e.dma_start(out=o_t1[:, :], in_=w_outT_in[di * 128:(di + 1) * 128, :]).then_inc(LD, 16)
            P.on("sync", lwo)
            ldv = P.inc("LD", 16)

            def cwo(v, di=di, ldv=ldv):
                v.wait_ge(LD, ldv)
                v.tensor_copy(o_wout[:, di * 1024:(di + 1) * 1024], o_t1[:, :])
                v.drain()
                v.engine_nop().then_inc(VEC, 1)
            P.on("vector", cwo)
            P.inc("VEC", 1)
            prev_cast = P.val("VEC")

            def pace(e, prev_cast=prev_cast):
                e.wait_ge(VEC, prev_cast)
            P.on("sync", pace)
        v_wo = P.val("VEC")

        def lyz_all(e):
            if not do_out:
                return
            e.wait_ge(CC, cc_yz)
            e.wait_ge(VEC, vec_scan_end)
            tok0 = e.compute_val(e.partition_id() * TOK)
            for di in range(16):
                e.dma_start(out=o_yzT[:, di * TOK:(di + 1) * TOK],
                            in_=yz_ag_out.ap()[di * 128:(di + 1) * 128,
                                               bass.ds(tok0, TOK)]).then_inc(LD2, 16)
        if do_out:
            P.on("gpsimd", lyz_all)
            ld_yz_all = P.inc("LD2", 16 * 16)

        for tt in (range(NTT) if do_out else []):
            prev_fin = P.val("VEC")

            def lyz(e, tt=tt, prev_fin=prev_fin):
                if tt >= 1:
                    e.wait_ge(VEC, prev_fin)
                e.dma_start(out=o_x[:, :], in_=x_in[tt * 128:(tt + 1) * 128, :]).then_inc(LD2, 16)
                e.dma_start(out=o_g[:, :],
                            in_=bass.AP(g_scr.ap().tensor, tt * 128 * D_MODEL,
                                        [[D_MODEL, 128], [1, D_MODEL]])).then_inc(LD2, 16)
            P.on("gpsimd", lyz)
            ldv = P.inc("LD2", 16 * 2)

            mmo_gate = P.val("VEC")

            def mmo(t, ldv=ldv, v_wo=v_wo, tt=tt, mmo_gate=mmo_gate):
                t.wait_ge(LD2, max(ldv, ld_yz_all))
                t.wait_ge(VEC, v_wo)
                if tt > 0:
                    t.wait_ge(VEC, mmo_gate)
                last = None
                for half in range(2):
                    for di in range(16):
                        last = t.matmul(ps_big[:, half * 512:(half + 1) * 512],
                                        o_yzT[:, di * TOK + tt * 128: di * TOK + (tt + 1) * 128],
                                        o_wout[:, di * 1024 + half * 512: di * 1024 + (half + 1) * 512],
                                        start=(di == 0), stop=(di == 15))
                last.then_inc(TEN, 1)
            P.on("tensor", mmo)
            tv_o = P.inc("TEN", 1)

            def fin(v, tv_o=tv_o, tt=tt):
                v.wait_ge(TEN, tv_o)
                v.tensor_tensor(o_t1[:, :], ps_big[:, :], o_g[:, :], ALU.mult)
                v.drain()
                v.tensor_tensor(o_t1[:, :], o_t1[:, :], o_x[:, :], ALU.add)
                v.drain()
                v.engine_nop().then_inc(VEC, 1)
            P.on("vector", fin)
            v_fin = P.inc("VEC", 1)

            def outst(g, tt=tt, v_fin=v_fin):
                g.wait_ge(VEC, v_fin)
                g.dma_start(out=out[tt * 128:(tt + 1) * 128, :],
                            in_=o_t1[:, :]).then_inc(ST, 16)
            P.on("gpsimd", outst)
            P.inc("ST", 16)
        if do_out:
            fin_st = P.val("ST")

            def waitall(g):
                g.wait_ge(ST, fin_st)
            P.on("gpsimd", waitall)

    if stage == "ffn":
        def fin_cp(e):
            e.wait_ge(ST, g_all)
            e.wait_ge(CC, cc_xn)
            for i in range(NTT):
                e.dma_start(out=out[i * 128:(i + 1) * 128, :],
                            in_=bass.AP(g_scr.ap().tensor, i * 128 * D_MODEL,
                                        [[D_MODEL, 128], [1, D_MODEL]])).then_inc(ST, 16)
        P.on("sync", fin_cp)
        fin = P.inc("ST", 16 * NTT)

        def fin_wait(g):
            g.wait_ge(ST, fin)
        P.on("gpsimd", fin_wait)

    with nc.Block() as block:
        @block.sync
        def _(e):
            for f in P.q["sync"]:
                f(e)

        @block.scalar
        def _(e):
            for f in P.q["scalar"]:
                f(e)

        @block.vector
        def _(e):
            for f in P.q["vector"]:
                f(e)

        @block.tensor
        def _(e):
            for f in P.q["tensor"]:
                f(e)

        @block.gpsimd
        def _(e):
            for f in P.q["gpsimd"]:
                f(e)
    ctx.close()
    return nc


def shard_inputs(inputs):
    x = np.ascontiguousarray(np.asarray(inputs["L0_attn_x"]).reshape(L, D_MODEL))
    wg = np.asarray(inputs["L0_ffn_w_gate"])
    wu = np.asarray(inputs["L0_ffn_w_up"])
    wd = np.asarray(inputs["L0_ffn_w_down"])
    ident = np.eye(128, dtype=np.float32)
    mask = np.zeros((128, 3), np.int32)
    mask[:, 0] = 0x7FFFFFFF
    mask[:, 1] = np.int32(np.uint32(0x80000000).view(np.int32))
    mask[:, 2] = 0x3F800000
    fc = np.zeros((128, 4), np.float32)
    fc[:, 0] = EPS
    fc[:, 1] = 1.0
    fc[:, 2] = -1.0
    fc[:, 3] = 0.5
    w_in = np.asarray(inputs["L0_attn_in_proj_w"])
    conv_w = np.asarray(inputs["L0_attn_conv_w"])[:, 0, :]
    conv_b = np.asarray(inputs["L0_attn_conv_b"])
    xpw = np.asarray(inputs["L0_attn_x_proj_w"])
    dtw = np.asarray(inputs["L0_attn_dt_proj_w"])[:, 0]
    dtb = np.asarray(inputs["L0_attn_dt_proj_b"])
    alog = np.asarray(inputs["L0_attn_A_log"])
    dvec = np.asarray(inputs["L0_attn_D"])
    w_out = np.asarray(inputs["L0_attn_out_proj_w"])
    w_outT = np.ascontiguousarray(w_out.T)

    def colpair(vec, k):
        return np.ascontiguousarray(vec[k * DI:(k + 1) * DI].reshape(2, 128).T)

    maps = []
    for k in range(NCORES):
        winT = np.ascontiguousarray(
            np.concatenate([w_in[k * DI:(k + 1) * DI],
                            w_in[D_INNER + k * DI:D_INNER + (k + 1) * DI]], 0).T)
        maps.append({
            "x_in": np.ascontiguousarray(x[k * TOK:(k + 1) * TOK]),
            "ident_in": ident,
            "mask_in": mask,
            "fc_in": fc,
            "wg_sl": np.ascontiguousarray(wg[k * IGR:(k + 1) * IGR]),
            "wu_sl": np.ascontiguousarray(wu[k * IGR:(k + 1) * IGR]),
            "wd_sl": np.ascontiguousarray(wd[k * DGR:(k + 1) * DGR]),
            "w_inT_sl": winT,
            "xpw_sl": np.ascontiguousarray(xpw.T[k * DI:(k + 1) * DI]),
            "conv_sl": np.ascontiguousarray(conv_w[k * DI:(k + 1) * DI]),
            "convb_sl": colpair(conv_b, k),
            "d_sl": colpair(dvec, k),
            "dtw_sl": colpair(dtw, k),
            "dtb_sl": colpair(dtb, k),
            "alog_sl": np.ascontiguousarray(alog[k * DI:(k + 1) * DI]),
            "w_outT_in": w_outT,
        })
    return maps


def kernel(**inputs):
    nc = build(stage="full", dev=False)
    maps = shard_inputs(inputs)
    res = run_bass_kernel_spmd(nc, maps, core_ids=list(range(NCORES)))
    outp = np.concatenate([r["out"] for r in res.results], axis=0)
    return outp.reshape(1, L, D_MODEL)


# version 10
# speedup vs baseline: 1.2101x; 1.2101x over previous
import numpy as np
import ml_dtypes
import concourse.bass as bass
import concourse.mybir as mybir
from concourse.bass_utils import run_bass_kernel_spmd

F32 = mybir.dt.float32
BF16 = mybir.dt.bfloat16
I32 = mybir.dt.int32
AF = mybir.ActivationFunctionType
ALU = mybir.AluOpType
AX = mybir.AxisListType

NCORES = 8
D_MODEL = 1024
D_INNER = 2048
N_ST = 16
L = 16384
I_FFN = 2816
TOK = L // NCORES
DI = D_INNER // NCORES
NTT = TOK // 128
NIT = I_FFN // 128
EPS = 1e-5
BF = np.dtype(ml_dtypes.bfloat16)
C6 = 1.0 / 6.0

IGR = I_FFN // 8
DGR = D_MODEL // 8
SZ_G = IGR * D_MODEL
SZ_D = DGR * I_FFN
WQ_ROW = 2 * SZ_G + SZ_D


class Prog:
    ENGS = ("sync", "scalar", "vector", "tensor", "gpsimd")

    def __init__(self):
        self.q = {e: [] for e in self.ENGS}
        self.cnt = {}

    def on(self, eng, f):
        self.q[eng].append(f)

    def inc(self, sem, k=1):
        self.cnt[sem] = self.cnt.get(sem, 0) + k
        return self.cnt[sem]

    def val(self, sem):
        return self.cnt.get(sem, 0)


def build(stage="full", dev=False, p1n=None, skip=()):
    nc = bass.Bass("TRN2", target_bir_lowering=False, debug=False)
    P = Prog()

    def ein(name, shape, dt):
        return nc.dram_tensor(name, shape, dt, kind="ExternalInput").ap()

    def hbm(name, shape, dt):
        return nc.dram_tensor(name, shape, dt, kind="Internal")

    x_in = ein("x_in", [TOK, D_MODEL], F32)
    ident_in = ein("ident_in", [128, 128], F32)
    mask_in = ein("mask_in", [128, 3], I32)
    fc_in = ein("fc_in", [128, 4], F32)
    wg_sl = ein("wg_sl", [IGR, D_MODEL], F32)
    w_inT_sl = ein("w_inT_sl", [D_MODEL, 4 * 128], F32)
    xpw_sl = ein("xpw_sl", [DI, 33], F32)
    conv_sl = ein("conv_sl", [DI, 4], F32)
    convb_sl = ein("convb_sl", [128, 2], F32)
    d_sl = ein("d_sl", [128, 2], F32)
    dtw_sl = ein("dtw_sl", [128, 2], F32)
    dtb_sl = ein("dtb_sl", [128, 2], F32)
    alog_sl = ein("alog_sl", [DI, N_ST], F32)
    w_outT_in = ein("w_outT_in", [D_INNER, D_MODEL], F32)
    wu_sl = ein("wu_sl", [IGR, D_MODEL], F32)
    wd_sl = ein("wd_sl", [DGR, I_FFN], F32)

    out = nc.dram_tensor("out", [TOK, D_MODEL], F32, kind="ExternalOutput").ap()
    dv = {}
    if dev:
        for nm, sh, dt in [("xn_dev", [TOK, D_MODEL], F32),
                           ("xnq_dev", [TOK, D_MODEL], F32),
                           ("hq_dev", [TOK, I_FFN], F32),
                           ("gate_dev", [128, I_FFN], F32),
                           ("g_dev", [TOK, D_MODEL], F32)]:
            dv[nm] = nc.dram_tensor(nm, sh, dt, kind="ExternalOutput").ap()

    wq_ag_in = hbm("wq_ag_in", [1, WQ_ROW], BF16)
    wq_ag_out = hbm("wq_ag_out", [NCORES, WQ_ROW], BF16)
    xnT_ag_in = hbm("xnT_ag_in", [D_MODEL, TOK], BF16)
    xnT_ag_out = hbm("xnT_ag_out", [NCORES * D_MODEL, TOK], BF16)
    hq_scr = hbm("hq_scr", [TOK, I_FFN], BF16)
    g_scr = hbm("g_scr", [TOK, D_MODEL], F32)
    xc_scr = hbm("xc_scr", [DI, L], F32)
    z_scr = hbm("z_scr", [DI, L], F32)
    xp_ar_in = hbm("xp_ar_in", [33, L], F32)
    xp_ar_out = hbm("xp_ar_out", [33, L], F32)
    dtr_rep_h = hbm("dtr_rep_h", [128, L], F32)
    yz_ag_in = hbm("yz_ag_in", [DI, L], BF16)
    yz_ag_inq = [hbm(f"yz_ag_inq{q}", [DI, L // 8], BF16) for q in range(8)]
    yz_ag_out = hbm("yz_ag_out", [8, D_INNER * (L // 8)], BF16)

    ARENA = 200 * 1024
    arena = nc.alloc_sbuf_tensor("myarena", [128, ARENA], mybir.dt.uint8)
    SB_BASE = nc.lookup_mloc(arena).addr
    ptr = [SB_BASE]

    def sb(name, shape, dt, at=None):
        per = int(np.prod(shape[1:])) * mybir.dt.size(dt)
        per = (per + 31) & ~31
        if at is None:
            at = ptr[0]
            ptr[0] = at + per
        assert at + per <= SB_BASE + ARENA, (name, at - SB_BASE, per)
        return nc.alloc_sbuf_tensor_at(name, shape, dt, offset=at)

    ident = sb("ident", [128, 128], F32)
    identb = sb("identb", [128, 128], BF16)
    maskc = sb("maskc", [128, 3], I32)
    fconst = sb("fconst", [128, 4], F32)
    amax_t = sb("amax_t", [128, 176], F32)
    sc_t = sb("sc_t", [128, 176], F32)
    rs_t = sb("rs_t", [128, 176], F32)
    hs_t = sb("hs_t", [128, 176], F32)
    ln_a = sb("ln_a", [128, 1], F32)
    ln_b = sb("ln_b", [128, 1], F32)
    ln_c = sb("ln_c", [128, 1], F32)
    ln_d = sb("ln_d", [128, 1], F32)
    PERM_END = ptr[0]

    xt_a_off = ptr[0]
    xt_a = sb("xt_a", [128, D_MODEL], F32)
    xt_b = sb("xt_b", [128, D_MODEL], F32)
    xn_a = sb("xn_a", [128, D_MODEL], F32)
    q_u = sb("q_u", [128, I_FFN], F32)
    q_t1 = sb("q_t1", [128, I_FFN], F32)
    q_t2 = sb("q_t2", [128, I_FFN], F32)
    q_t3 = sb("q_t3", [128, I_FFN], F32)
    xq_a = sb("xq_a", [128, D_MODEL], BF16)
    xnb_a = sb("xnb_a", [128, D_MODEL], BF16)
    evac_a = sb("evac_a", [128, 128], BF16)
    evac_b = sb("evac_b", [128, 128], BF16)
    xnqT = sb("xnqT", [128, 8 * TOK], BF16)
    wqs_b = sb("wqs_b", [128, I_FFN], BF16)
    FW = ptr[0]
    wgT = sb("wgT", [128, 8 * I_FFN], BF16)
    wuT = sb("wuT", [128, 8 * I_FFN], BF16)
    ffn_t1 = sb("ffn_t1", [128, 1024], F32)
    hq_a = sb("hq_a", [128, I_FFN], BF16)
    A_END = ptr[0]
    wsl_a = sb("wsl_a", [128, I_FFN], F32, at=FW)
    hu_b = sb("hu_b", [128, I_FFN], F32, at=SB_BASE + PERM_END - SB_BASE if False else xt_a_off)
    wdT = sb("wdT", [128, NIT * D_MODEL], BF16, at=FW)
    hqT_b = sb("hqT_b", [128, NIT * 128], BF16, at=FW + NIT * D_MODEL * 2)
    hq_in = sb("hq_in", [128, I_FFN], BF16, at=FW + NIT * D_MODEL * 2 + NIT * 256)
    gout_b = sb("gout_b", [128, D_MODEL], F32,
                at=FW + NIT * D_MODEL * 2 + NIT * 256 + I_FFN * 2)

    ps_big = nc.alloc_psum_tensor("ps_big", [128, 1024], F32)
    ps_a = nc.alloc_psum_tensor("ps_a", [128, 512], F32)
    ps_b = nc.alloc_psum_tensor("ps_b", [128, 512], F32)
    ps_t1 = nc.alloc_psum_tensor("ps_t1", [128, 128], BF16)
    ps_t2 = nc.alloc_psum_tensor("ps_t2", [128, 128], BF16)
    ps_xp = nc.alloc_psum_tensor("ps_xp", [33, 512], F32)
    ps_sp = nc.alloc_psum_tensor("ps_sp", [128, 512], F32)

    import contextlib
    ctx = contextlib.ExitStack()
    sem = {}
    for s in ("LD", "ACTS", "VEC", "TEN", "POOL", "ST", "CC", "LD2", "LD3"):
        sem[s] = ctx.enter_context(nc.semaphore(s))
    LD, ACTS, VEC, TEN, POOL, ST, CC, LD2, LD3 = (sem[s] for s in
        ("LD", "ACTS", "VEC", "TEN", "POOL", "ST", "CC", "LD2", "LD3"))

    def quant(v, src, dst, W, ub=None):
        ub = ub if ub is not None else q_u
        nb = W // 16
        prow_s = src.ap().ap[0][0]
        av = bass.AP(src.ap().tensor, 0, [[prow_s, 128], [16, nb], [1, 16]])
        v.tensor_reduce(amax_t[:, :nb], av, AX.X, ALU.max, apply_absolute_value=True)
        v.drain()
        v.tensor_scalar(sc_t[:, :nb], amax_t[:, :nb], C6, 1e-12, ALU.mult, ALU.max)
        v.drain()
        v.reciprocal(rs_t[:, :nb], sc_t[:, :nb])
        v.tensor_scalar(hs_t[:, :nb], sc_t[:, :nb], 0.5, None, ALU.mult)
        v.drain()
        rsb = bass.AP(rs_t.ap().tensor, 0, [[rs_t.ap().ap[0][0], 128], [1, nb], [0, 16]])
        uv = bass.AP(ub.ap().tensor, 0, [[ub.ap().ap[0][0], 128], [16, nb], [1, 16]])
        v.tensor_tensor(uv, av, rsb, ALU.mult)
        ui = ub[:, :W].bitcast(I32)
        t1i = q_t1[:, :W].bitcast(I32)
        v.tensor_scalar(t1i, ui, maskc[:, 0:1], None, ALU.bitwise_and)
        uu = q_t1[:, :W]
        v.tensor_scalar(q_t2[:, :W], uu, 0.25, None, ALU.is_gt)
        v.scalar_tensor_tensor(q_t2[:, :W], uu, 0.75, q_t2[:, :W], ALU.is_gt, ALU.add)
        v.scalar_tensor_tensor(q_t2[:, :W], uu, 1.25, q_t2[:, :W], ALU.is_gt, ALU.add)
        v.scalar_tensor_tensor(q_t2[:, :W], uu, 1.75, q_t2[:, :W], ALU.is_gt, ALU.add)
        v.tensor_scalar(q_t3[:, :W], uu, 2.5, None, ALU.is_gt)
        v.scalar_tensor_tensor(q_t3[:, :W], uu, 3.5, q_t3[:, :W], ALU.is_gt, ALU.add)
        v.tensor_scalar(uu, uu, 5.0, 4.0, ALU.is_gt, ALU.mult)
        v.scalar_tensor_tensor(q_t3[:, :W], q_t3[:, :W], 2.0, uu, ALU.mult, ALU.add)
        v.tensor_tensor(q_t2[:, :W], q_t2[:, :W], q_t3[:, :W], ALU.add)
        t3i = q_t3[:, :W].bitcast(I32)
        v.tensor_scalar(t3i, ui, maskc[:, 1:2], maskc[:, 2:3],
                        ALU.bitwise_and, ALU.bitwise_or)
        v.tensor_tensor(q_t2[:, :W], q_t2[:, :W], q_t3[:, :W], ALU.mult)
        hsb = bass.AP(hs_t.ap().tensor, 0, [[hs_t.ap().ap[0][0], 128], [1, nb], [0, 16]])
        qv = bass.AP(q_t2.ap().tensor, 0, [[q_t2.ap().ap[0][0], 128], [16, nb], [1, 16]])
        dv3 = bass.AP(dst.ap().tensor, 0, [[dst.ap().ap[0][0], 128], [16, nb], [1, 16]])
        v.tensor_tensor(dv3, qv, hsb, ALU.mult)

    def ld_const(e):
        e.dma_start(out=ident[:, :], in_=ident_in).then_inc(LD, 16)
        e.dma_start(out=maskc[:, :], in_=mask_in).then_inc(LD, 16)
        e.dma_start(out=fconst[:, :], in_=fc_in).then_inc(LD, 16)
    P.on("sync", ld_const)
    P.inc("LD", 48)

    def cast_ident(v):
        v.wait_ge(LD, 48)
        v.tensor_copy(identb[:, :], ident[:, :]).then_inc(VEC, 1)
    P.on("vector", cast_ident)
    v_ident = P.inc("VEC", 1)

    def wslice(src_ap, rows, width, dst_off):
        for it in range((rows + 127) // 128):
            r0, r1 = it * 128, min((it + 1) * 128, rows)
            rr = r1 - r0
            prev_vec = P.val("VEC")

            def ldw(e, r0=r0, r1=r1, width=width, prev_vec=prev_vec):
                e.wait_ge(VEC, prev_vec)
                e.dma_start(out=wsl_a[:r1 - r0, :width],
                            in_=src_ap[r0:r1, :]).then_inc(LD, 16)
            P.on("sync", ldw)
            ldv = P.inc("LD", 16)

            def qw(v, rr=rr, width=width, ldv=ldv):
                v.wait_ge(LD, ldv)
                quant(v, wsl_a, wqs_b, width)
                v.engine_nop().then_inc(VEC, 1)
            P.on("vector", qw)
            qv = P.inc("VEC", 1)

            def stw(g, r0=r0, rr=rr, width=width, qv=qv, dst_off=dst_off):
                g.wait_ge(VEC, qv)
                src = bass.AP(wqs_b.ap().tensor, 0,
                              [[wqs_b.ap().ap[0][0], rr], [1, width]])
                dst = bass.AP(wq_ag_in.ap().tensor, dst_off + r0 * width,
                              [[width, rr], [1, width]])
                g.dma_start(out=dst, in_=src).then_inc(ST, 16)
            P.on("gpsimd", stw)
            P.inc("ST", 16)

    wslice(wg_sl, IGR, D_MODEL, 0)
    wslice(wu_sl, IGR, D_MODEL, SZ_G)
    wslice(wd_sl, DGR, I_FFN, 2 * SZ_G)
    wq_staged = P.val("ST")

    def ag_wq(g):
        g.wait_ge(ST, wq_staged)
        g.collective_compute("AllGather", ALU.bypass,
                             replica_groups=[list(range(NCORES))],
                             ins=[wq_ag_in.ap().opt()],
                             outs=[wq_ag_out.ap().opt()]).then_inc(CC, 1)
    P.on("gpsimd", ag_wq)
    cc_wq = P.inc("CC", 1)

    def load_wT(dst, piece_off, rows, rows_per_core, width):
        nrt = rows // 128
        nkt = width // 128
        for rt in range(nrt):
            g0 = rt * 128
            pieces = []
            while g0 < (rt + 1) * 128:
                c = g0 // rows_per_core
                take = min((c + 1) * rows_per_core, (rt + 1) * 128) - g0
                pieces.append((g0 - rt * 128, c, g0 - c * rows_per_core, take))
                g0 += take
            prev_t = P.val("TEN")

            def ldw(e, pieces=pieces, piece_off=piece_off, width=width, prev_t=prev_t):
                e.wait_ge(TEN, prev_t)
                for (po, c, rc, take) in pieces:
                    src = bass.AP(wq_ag_out.ap().tensor,
                                  c * WQ_ROW + piece_off + rc * width,
                                  [[width, take], [1, width]])
                    e.dma_start(out=wqs_b[po:po + take, :width],
                                in_=src).then_inc(LD2, 16)
            P.on("sync", ldw)
            ldv = P.inc("LD2", 16 * len(pieces))

            for kk in range(nkt):
                slot = [ps_t1, ps_t2][kk % 2]
                marks = P.wt_marks
                need = marks[-2] if len(marks) >= 2 else 0

                def tp(t, kk=kk, ldv=ldv, need=need, slot=slot):
                    t.wait_ge(LD2, ldv)
                    if need:
                        t.wait_ge(POOL, need)
                    t.matmul(slot[:, :], wqs_b[:, kk * 128:(kk + 1) * 128],
                             identb[:, :], is_transpose=True,
                             start=True, stop=True).then_inc(TEN, 1)
                P.on("tensor", tp)
                tv = P.inc("TEN", 1)

                def ev(sc, dst=dst, kk=kk, rt=rt, rows=rows, tv=tv, slot=slot):
                    sc.wait_ge(TEN, tv)
                    sc.copy(dst[:, kk * rows + rt * 128:
                                kk * rows + (rt + 1) * 128],
                            slot[:, :])
                    sc.drain()
                    sc.nop().then_inc(POOL, 1)
                P.on("scalar", ev)
                marks.append(P.inc("POOL", 1))

    def gate_cc(e):
        e.wait_ge(CC, cc_wq)
    P.on("sync", gate_cc)
    P.wt_marks = []
    load_wT(wgT, 0, I_FFN, IGR, D_MODEL)
    load_wT(wuT, SZ_G, I_FFN, IGR, D_MODEL)
    wT_pool = P.val("POOL")

    xbufs = [xt_a, xt_b]
    xn_tile_vec = []
    xnqT_pool = []
    ev_slot_st = {}
    P.x_marks = []
    for i in range(NTT):
        pace = xn_tile_vec[i - 2] if i >= 2 else 0

        def ldx(e, i=i, pace=pace):
            if pace:
                e.wait_ge(VEC, pace)
            e.dma_start(out=xbufs[i % 2][:, :],
                        in_=x_in[i * 128:(i + 1) * 128, :]).then_inc(LD, 16)
        P.on("scalar", ldx)
        ldv = P.inc("LD", 16)
        xb = xbufs[i % 2]
        prev_q = xn_tile_vec[-1] if xn_tile_vec else 0

        def stats(s, xb=xb, ldv=ldv, prev_q=prev_q):
            s.wait_ge(LD, ldv)
            s.wait_ge(VEC, prev_q)
            s.activation(q_t1[:, :D_MODEL], xb[:, :], AF.Copy,
                         bias=0.0, scale=1.0, accum_out=ln_a[:, :])
            s.activation(q_t2[:, :D_MODEL], xb[:, :], AF.Square,
                         bias=0.0, scale=1.0, accum_out=ln_b[:, :]).then_inc(ACTS, 1)
        P.on("scalar", stats)
        a1 = P.inc("ACTS", 1)

        def smalls(v, a1=a1):
            v.wait_ge(ACTS, a1)
            v.tensor_scalar(ln_a[:, :], ln_a[:, :], 1.0 / D_MODEL, None, ALU.mult)
            v.tensor_scalar(ln_b[:, :], ln_b[:, :], 1.0 / D_MODEL, None, ALU.mult)
            v.drain()
            v.tensor_tensor(ln_c[:, :], ln_a[:, :], ln_a[:, :], ALU.mult)
            v.drain()
            v.tensor_tensor(ln_b[:, :], ln_b[:, :], ln_c[:, :], ALU.subtract)
            v.drain()
            v.engine_nop().then_inc(VEC, 1)
        P.on("vector", smalls)
        v1 = P.inc("VEC", 1)

        def rstd(s, v1=v1):
            s.wait_ge(VEC, v1)
            s.activation(ln_c[:, :], ln_b[:, :], AF.Sqrt, bias=fconst[:, 0:1], scale=1.0).then_inc(ACTS, 1)
        P.on("scalar", rstd)
        a2 = P.inc("ACTS", 1)

        def nbias(v, a2=a2):
            v.wait_ge(ACTS, a2)
            v.reciprocal(ln_c[:, :], ln_c[:, :])
            v.drain()
            v.tensor_tensor(ln_d[:, :], ln_a[:, :], ln_c[:, :], ALU.mult)
            v.drain()
            v.tensor_scalar(ln_d[:, :], ln_d[:, :], -1.0, None, ALU.mult)
            v.drain()
            v.engine_nop().then_inc(VEC, 1)
        P.on("vector", nbias)
        v2 = P.inc("VEC", 1)

        def qtile(v, xb=xb):
            v.tensor_scalar(xn_a[:, :], xb[:, :], ln_c[:, :], ln_d[:, :],
                            ALU.mult, ALU.add)
            quant(v, xn_a, xq_a, D_MODEL)
            v.tensor_copy(xnb_a[:, :], xn_a[:, :]).then_inc(VEC, 1)
        P.on("vector", qtile)
        v3 = P.inc("VEC", 1)
        xn_tile_vec.append(v3)

        if dev:
            def devtap(g, i=i, v3=v3):
                g.wait_ge(VEC, v3)
                g.dma_start(out=dv["xn_dev"][i * 128:(i + 1) * 128, :],
                            in_=xn_a[:, :]).then_inc(ST, 16)
                g.dma_start(out=dv["xnq_dev"][i * 128:(i + 1) * 128, :],
                            in_=xq_a[:, :]).then_inc(ST, 16)
            P.on("gpsimd", devtap)
            P.inc("ST", 32)

        for kk in range(8):
            psq = ps_t1
            psn = ps_t2
            xm = P.x_marks
            prev_pool = xm[-1] if len(xm) >= 1 else 0

            def tp(t, kk=kk, v3=v3, psq=psq, psn=psn, prev_pool=prev_pool):
                t.wait_ge(VEC, v3)
                if prev_pool:
                    t.wait_ge(POOL, prev_pool)
                t.matmul(psq[:, :], xq_a[:, kk * 128:(kk + 1) * 128], identb[:, :],
                         is_transpose=True, start=True, stop=True)
                t.matmul(psn[:, :], xnb_a[:, kk * 128:(kk + 1) * 128], identb[:, :],
                         is_transpose=True, start=True, stop=True).then_inc(TEN, 1)
            P.on("tensor", tp)
            tv = P.inc("TEN", 1)
            eb = evac_a if kk % 2 == 0 else evac_b
            st_need = ev_slot_st.get(kk % 2, 0)

            def ev(sc, kk=kk, i=i, tv=tv, psq=psq, psn=psn, eb=eb, st_need=st_need):
                sc.wait_ge(TEN, tv)
                if st_need > 0:
                    sc.wait_ge(ST, st_need)
                sc.copy(xnqT[:, kk * TOK + i * 128: kk * TOK + (i + 1) * 128],
                        psq[:, :])
                sc.copy(eb[:, :], psn[:, :])
                sc.drain()
                sc.nop().then_inc(POOL, 1)
            P.on("scalar", ev)
            pv = P.inc("POOL", 1)
            P.x_marks.append(pv)

            def evd(g, kk=kk, i=i, eb=eb, pv=pv):
                g.wait_ge(POOL, pv)
                g.dma_start(out=xnT_ag_in.ap()[kk * 128:(kk + 1) * 128,
                                               i * 128:(i + 1) * 128],
                            in_=eb[:, :]).then_inc(ST, 16)
            P.on("gpsimd", evd)
            ev_slot_st[kk % 2] = P.inc("ST", 16)
        xnqT_pool.append(P.val("POOL"))

    xnT_staged = P.val("ST")

    def ag_xn(g):
        g.wait_ge(ST, xnT_staged)
        g.collective_compute("AllGather", ALU.bypass,
                             replica_groups=[list(range(NCORES))],
                             ins=[xnT_ag_in.ap().opt()],
                             outs=[xnT_ag_out.ap().opt()]).then_inc(CC, 1)
    P.on("gpsimd", ag_xn)
    cc_xn = P.inc("CC", 1)

    chunks = [(c * 512, 512) for c in range(5)] + [(2560, 256)]
    hmul_marks = []
    for i in range(NTT):
        tile_pool = xnqT_pool[i]
        for ci, (c0, cw) in enumerate(chunks):
            gi = len(hmul_marks)
            par = gi % 2
            pg = ps_a if par == 0 else ps_sp
            pu = ps_b if par == 0 else ps_big
            prev_hmul = hmul_marks[-2] if len(hmul_marks) >= 2 else 0

            def mm(t, i=i, c0=c0, cw=cw, tile_pool=tile_pool, prev_hmul=prev_hmul,
                   pg=pg, pu=pu):
                t.wait_ge(POOL, max(wT_pool, tile_pool))
                if prev_hmul:
                    t.wait_ge(VEC, prev_hmul)
                for kk in range(8):
                    t.matmul(pg[:, :cw],
                             xnqT[:, kk * TOK + i * 128: kk * TOK + (i + 1) * 128],
                             wgT[:, kk * I_FFN + c0: kk * I_FFN + c0 + cw],
                             start=(kk == 0), stop=(kk == 7))
                last = None
                for kk in range(8):
                    last = t.matmul(pu[:, :cw],
                                    xnqT[:, kk * TOK + i * 128: kk * TOK + (i + 1) * 128],
                                    wuT[:, kk * I_FFN + c0: kk * I_FFN + c0 + cw],
                                    start=(kk == 0), stop=(kk == 7))
                last.then_inc(TEN, 1)
            P.on("tensor", mm)
            tv = P.inc("TEN", 1)

            def sil(s, i=i, c0=c0, cw=cw, tv=tv, pg=pg, par=par):
                s.wait_ge(TEN, tv)
                s.activation(ffn_t1[:, par * 512:par * 512 + cw], pg[:, :cw], AF.Silu,
                             bias=0.0, scale=1.0).then_inc(ACTS, 1)
            P.on("scalar", sil)
            av = P.inc("ACTS", 1)

            def hmul(v, c0=c0, cw=cw, av=av, pu=pu, par=par):
                v.wait_ge(ACTS, av)
                v.tensor_tensor(q_u[:, c0:c0 + cw], ffn_t1[:, par * 512:par * 512 + cw],
                                pu[:, :cw], ALU.mult).then_inc(VEC, 1)
            P.on("vector", hmul)
            hmul_marks.append(P.inc("VEC", 1))

        def qh2(v):
            quant(v, q_u, hq_a, I_FFN, ub=hu_b)
            v.engine_nop().then_inc(VEC, 1)
        P.on("vector", qh2)
        vq = P.inc("VEC", 1)

        def sth(g, i=i, vq=vq):
            g.wait_ge(VEC, vq)
            g.dma_start(out=hq_scr.ap()[i * 128:(i + 1) * 128, :],
                        in_=hq_a[:, :]).then_inc(ST, 16)
            if dev:
                g.dma_start(out=dv["hq_dev"][i * 128:(i + 1) * 128, :],
                            in_=hq_a[:, :]).then_inc(ST, 16)
                if i == 0:
                    g.dma_start(out=dv["gate_dev"][:, :],
                                in_=q_t3[:, :I_FFN]).then_inc(ST, 16)
        P.on("gpsimd", sth)
        P.inc("ST", 16 + (32 if dev else 0) if False else 16)
        if dev:
            P.inc("ST", 16 + (16 if i == 0 else 0))
    hq_all = P.val("ST")
    ffn_gu_ten = P.val("TEN")

    P.wd_marks = []
    for rt in range(8):
        prev_t = P.val("TEN")

        def ldw(e, rt=rt, prev_t=prev_t):
            e.wait_ge(TEN, max(prev_t, ffn_gu_ten))
            src = bass.AP(wq_ag_out.ap().tensor, rt * WQ_ROW + 2 * SZ_G,
                          [[I_FFN, 128], [1, I_FFN]])
            e.dma_start(out=wqs_b[:, :I_FFN], in_=src).then_inc(LD2, 16)
        P.on("sync", ldw)
        ldv = P.inc("LD2", 16)
        for it in range(NIT):
            slot = [ps_t1, ps_t2][it % 2]
            marks = P.wd_marks
            need = marks[-2] if len(marks) >= 2 else 0

            def tp(t, it=it, ldv=ldv, need=need, slot=slot):
                t.wait_ge(LD2, ldv)
                if need:
                    t.wait_ge(POOL, need)
                t.matmul(slot[:, :], wqs_b[:, it * 128:(it + 1) * 128],
                         identb[:, :], is_transpose=True,
                         start=True, stop=True).then_inc(TEN, 1)
            P.on("tensor", tp)
            tv = P.inc("TEN", 1)

            def ev(sc, it=it, rt=rt, tv=tv, slot=slot):
                sc.wait_ge(TEN, tv)
                sc.copy(wdT[:, it * D_MODEL + rt * 128:
                            it * D_MODEL + (rt + 1) * 128],
                        slot[:, :])
                sc.drain()
                sc.nop().then_inc(POOL, 1)
            P.on("scalar", ev)
            marks.append(P.inc("POOL", 1))
    wd_pool = P.val("POOL")

    P.hq_marks = []
    for i in range(NTT):
        prev_ten = P.val("TEN")

        def ldh(e, i=i, prev_ten=prev_ten):
            e.wait_ge(ST, hq_all)
            e.wait_ge(TEN, prev_ten)
            e.dma_start(out=hq_in[:, :],
                        in_=hq_scr.ap()[i * 128:(i + 1) * 128, :]).then_inc(LD2, 16)
        P.on("sync", ldh)
        ldv = P.inc("LD2", 16)
        for it in range(NIT):
            slot = [ps_t1, ps_t2][it % 2]
            marks = P.hq_marks
            need = marks[-2] if len(marks) >= 2 else 0

            def tp(t, it=it, ldv=ldv, need=need, slot=slot):
                t.wait_ge(LD2, ldv)
                if need:
                    t.wait_ge(POOL, need)
                t.matmul(slot[:, :], hq_in[:, it * 128:(it + 1) * 128],
                         identb[:, :], is_transpose=True,
                         start=True, stop=True).then_inc(TEN, 1)
            P.on("tensor", tp)
            tv = P.inc("TEN", 1)

            def ev(sc, it=it, tv=tv, slot=slot):
                sc.wait_ge(TEN, tv)
                sc.copy(hqT_b[:, it * 128:(it + 1) * 128],
                        slot[:, :])
                sc.drain()
                sc.nop().then_inc(POOL, 1)
            P.on("scalar", ev)
            marks.append(P.inc("POOL", 1))
        hT_pool = P.val("POOL")
        prev_gsil = P.val("ACTS")

        def dmm(t, hT_pool=hT_pool, prev_gsil=prev_gsil):
            t.wait_ge(POOL, max(hT_pool, wd_pool))
            t.wait_ge(ACTS, prev_gsil)
            last = None
            for half in range(2):
                for it in range(NIT):
                    last = t.matmul(ps_big[:, half * 512:(half + 1) * 512],
                                    hqT_b[:, it * 128:(it + 1) * 128],
                                    wdT[:, it * D_MODEL + half * 512:
                                        it * D_MODEL + (half + 1) * 512],
                                    start=(it == 0), stop=(it == NIT - 1))
            last.then_inc(TEN, 1)
        P.on("tensor", dmm)
        tv = P.inc("TEN", 1)

        def gsil(s, tv=tv):
            s.wait_ge(TEN, tv)
            s.activation(gout_b[:, :], ps_big[:, :], AF.Silu,
                         bias=0.0, scale=1.0).then_inc(ACTS, 1)
        P.on("scalar", gsil)
        av = P.inc("ACTS", 1)

        def gst(g, i=i, av=av):
            g.wait_ge(ACTS, av)
            g.dma_start(out=g_scr.ap()[i * 128:(i + 1) * 128, :],
                        in_=gout_b[:, :]).then_inc(ST, 16)
            if dev:
                g.dma_start(out=dv["g_dev"][i * 128:(i + 1) * 128, :],
                            in_=gout_b[:, :]).then_inc(ST, 16)
        P.on("gpsimd", gst)
        P.inc("ST", 16 + (16 if dev else 0))
    g_all = P.val("ST")


    vec_ffn = P.val("VEC")
    ten_ffn = P.val("TEN")
    act_ffn = P.val("ACTS")
    if stage in ("pass1", "scan", "mid", "full"):
        STB = 256
        NSB = L // STB
        MB = xt_a_off

        def msb(name, shape, dt, off):
            return sb(name, shape, dt, at=MB + off)

        m_rhs = msb("m_rhs", [128, 2 * 8 * 512], BF16, 0)
        m_winT = msb("m_winT", [128, 32 * 128], BF16, 16384)
        m_wst = msb("m_wst", [128, 512], F32, 24576)
        m_xs = msb("m_xs", [128, 4 * 520], F32, 26624)
        m_xc = msb("m_xc", [128, 2 * 512], F32, 35328)
        m_xcb = msb("m_xcb", [128, 2 * 512], BF16, 39424)
        m_zst = msb("m_zst", [128, 2 * 512], F32, 41472)
        m_xpst = msb("m_xpst", [33, 512], F32, 45568)
        m_xpw = msb("m_xpw", [128, 2 * 33], BF16, 47616)
        m_xpwf = msb("m_xpwf", [128, 2 * 33], F32, 47872)
        SCB = 49152
        s_bc = msb("s_bc", [128, 2 * 33 * STB], F32, SCB)
        s_xc2 = msb("s_xc2", [128, 4 * STB], F32, SCB + 67584)
        s_z2 = msb("s_z2", [128, 4 * STB], F32, SCB + 71680)
        s_dt = msb("s_dt", [128, 2 * STB], F32, SCB + 75776)
        s_U = msb("s_U", [128, 2 * STB], F32, SCB + 77824)
        s_dA = msb("s_dA", [128, 16 * STB], F32, SCB + 79872)
        s_dBx = msb("s_dBx", [128, 16 * STB], F32, SCB + 96256)
        s_h = msb("s_h", [128, 16 * STB], F32, SCB + 112640)
        s_hprev = msb("s_hprev", [128, 2 * 16], F32, SCB + 129024)
        s_A = msb("s_A", [128, 32], F32, SCB + 129280)
        s_Af = msb("s_Af", [128, 32], F32, SCB + 129536)
        s_zs = msb("s_zs", [128, 256], F32, SCB + 129792)
        s_yz = msb("s_yz", [128, 256], F32, SCB + 130816)
        s_yzT = msb("s_yzT", [128, 2 * STB], BF16, SCB + 131840)
        s_tmp = msb("s_tmp", [128, 512], F32, SCB + 132864)
        s_hc = msb("s_hc", [128, 16 * STB], F32, SCB + 134976)
        s_xcd = msb("s_xcd", [128, 4 * STB], F32, 0)
        OB = SCB
        o_wout = msb("o_wout", [128, 16 * 1024], BF16, OB)
        o_yzT = msb("o_yzT", [128, 16 * TOK], BF16, OB + 32768)
        o_x = msb("o_x", [128, D_MODEL], F32, OB + 32768 + 65536)
        o_g = msb("o_g", [128, D_MODEL], F32, OB + 32768 + 69632)
        o_t1 = msb("o_t1", [128, D_MODEL], F32, OB + 32768 + 73728)

        def mconst(e):
            e.wait_ge(ST, g_all)
            e.dma_start(out=m_xpwf[:, 0:33], in_=xpw_sl[0:128, :]).then_inc(LD, 16)
            e.dma_start(out=m_xpwf[:, 33:66], in_=xpw_sl[128:256, :]).then_inc(LD, 16)
            e.dma_start(out=s_Af[:, 0:16], in_=alog_sl[0:128, :]).then_inc(LD, 16)
            e.dma_start(out=s_Af[:, 16:32], in_=alog_sl[128:256, :]).then_inc(LD, 16)
            e.dma_start(out=s_tmp[:, 0:4], in_=conv_sl[0:128, :]).then_inc(LD, 16)
            e.dma_start(out=s_tmp[:, 4:8], in_=conv_sl[128:256, :]).then_inc(LD, 16)
            e.dma_start(out=s_tmp[:, 8:10], in_=convb_sl).then_inc(LD, 16)
            e.dma_start(out=s_tmp[:, 10:12], in_=d_sl).then_inc(LD, 16)
            e.dma_start(out=s_tmp[:, 12:14], in_=dtw_sl).then_inc(LD, 16)
            e.dma_start(out=s_tmp[:, 14:16], in_=dtb_sl).then_inc(LD, 16)
        P.on("sync", mconst)
        ld_mc = P.inc("LD", 160)
        m_cols = msb("m_cols", [128, 16], F32, SCB + 134912)
        def mccp(v):
            v.wait_ge(LD, ld_mc)
            v.tensor_copy(m_cols[:, :], s_tmp[:, 0:16])
            v.tensor_copy(m_xpw[:, 0:66], m_xpwf[:, 0:66])
            v.drain()
            v.engine_nop().then_inc(VEC, 1)
        P.on("vector", mccp)
        v_mc = P.inc("VEC", 1)

        def mA(sc):
            sc.wait_ge(LD, ld_mc)
            sc.activation(s_A[:, :], s_Af[:, :], AF.Exp, bias=0.0, scale=1.0).then_inc(ACTS, 1)
        P.on("scalar", mA)
        a_mA = P.inc("ACTS", 1)
        def mA2(v):
            v.wait_ge(ACTS, a_mA)
            v.tensor_scalar(s_A[:, :], s_A[:, :], -1.0, None, ALU.mult)
            v.drain()
            v.engine_nop().then_inc(VEC, 1)
        P.on("vector", mA2)
        v_mA = P.inc("VEC", 1)

        for kk in range(8):
            pace_v = P.val("VEC")

            def lw(e, kk=kk, pace_v=pace_v):
                e.wait_ge(VEC, pace_v)
                e.dma_start(out=m_wst[:, :], in_=w_inT_sl[kk * 128:(kk + 1) * 128, :]).then_inc(LD, 16)
            P.on("sync", lw)
            ldv = P.inc("LD", 16)
            def cw(v, kk=kk, ldv=ldv):
                v.wait_ge(LD, ldv)
                v.tensor_copy(m_winT[:, kk * 512:(kk + 1) * 512], m_wst[:, :])
                v.drain()
                v.engine_nop().then_inc(VEC, 1)
            P.on("vector", cw)
            P.inc("VEC", 1)
        v_wl = P.val("VEC")

        NTB = L // 512
        P.xpev_marks = {}
        P.evac_marks = {}
        if p1n is None:
            p1n = NTB
        conv_carry_vec = 0
        xp_st = 0
        for tb in range(p1n):
            c = tb // 4
            col0 = (tb % 4) * 512
            buf = tb % 2
            prev_ten = P.val("TEN")

            def ldr(e, c=c, col0=col0, buf=buf, tb=tb, prev_ten=prev_ten):
                if tb == 0:
                    e.wait_ge(CC, cc_xn)
                    e.wait_ge(VEC, v_wl)
                if tb >= 2:
                    e.wait_ge(TEN, prev_ten)
                for kk in range(8):
                    if "tbld" in skip:
                        e.dma_start(out=m_rhs[0:1, 0:8],
                                    in_=xnT_ag_out.ap()[0:1, 0:8]).then_inc(LD2, 16)
                        continue
                    e.dma_start(out=m_rhs[:, buf * 4096 + kk * 512: buf * 4096 + (kk + 1) * 512],
                                in_=bass.AP(xnT_ag_out.ap().tensor,
                                            (c * D_MODEL + kk * 128) * TOK + col0,
                                            [[TOK, 128], [1, 512]])).then_inc(LD2, 16)
            P.on("sync", ldr)
            ldv = P.inc("LD2", 128)

            psums = [ps_a[:, :], ps_b[:, :], ps_big[:, 0:512], ps_big[:, 512:1024]]
            prev_act = P.evac_marks.get(tb - 1, 0)
            prev_vec = P.val("VEC")

            def mmip(t, ldv=ldv, buf=buf, prev_act=prev_act, prev_vec=prev_vec, tb=tb):
                t.wait_ge(LD2, ldv)
                if tb > 0:
                    t.wait_ge(ACTS, prev_act)
                    t.wait_ge(VEC, prev_vec)
                last = None
                if "tbmm" not in skip:
                    for mi in range(4):
                        for kk in range(8):
                            last = t.matmul(psums[mi],
                                            m_winT[:, kk * 512 + mi * 128: kk * 512 + (mi + 1) * 128],
                                            m_rhs[:, buf * 4096 + kk * 512: buf * 4096 + (kk + 1) * 512],
                                            start=(kk == 0), stop=(kk == 7))
                (last if last is not None else t.nop()).then_inc(TEN, 1)
            P.on("tensor", mmip)
            tv_ip = P.inc("TEN", 1)

            def zev(sc, tv_ip=tv_ip):
                sc.wait_ge(TEN, tv_ip)
                sc.copy(m_zst[:, 0:512], ps_big[:, 0:512])
                sc.copy(m_zst[:, 512:1024], ps_big[:, 512:1024])
                sc.drain()
                sc.nop().then_inc(ACTS, 1)
            P.on("scalar", zev)
            a_z = P.inc("ACTS", 1)

            def zst(g, tb=tb, a_z=a_z):
                g.wait_ge(ACTS, a_z)
                if "tbz" in skip:
                    g.dma_start(out=z_scr.ap()[0:1, 0:8], in_=m_zst[0:1, 0:8]).then_inc(ST, 16)
                    g.dma_start(out=z_scr.ap()[0:1, 0:8], in_=m_zst[0:1, 0:8]).then_inc(ST, 16)
                    return
                g.dma_start(out=bass.AP(z_scr.ap().tensor, 0 * L + tb * 512,
                                        [[L, 128], [1, 512]]),
                            in_=m_zst[:, 0:512]).then_inc(ST, 16)
                g.dma_start(out=bass.AP(z_scr.ap().tensor, 128 * L + tb * 512,
                                        [[L, 128], [1, 512]]),
                            in_=m_zst[:, 512:1024]).then_inc(ST, 16)
            P.on("gpsimd", zst)
            P.inc("ST", 32)

            for dg in range(2):
                xs_cur = m_xs[:, (buf * 2 + dg) * 520: (buf * 2 + dg) * 520 + 515]
                xs_prv = m_xs[:, ((1 - buf) * 2 + dg) * 520: ((1 - buf) * 2 + dg) * 520 + 515]

                def cvev(sc, dg=dg, tv_ip=tv_ip, xs_cur=xs_cur, xs_prv=xs_prv, tb=tb,
                         prev_vec=prev_vec):
                    sc.wait_ge(TEN, tv_ip)
                    if tb > 0:
                        sc.wait_ge(VEC, prev_vec)
                    sc.copy(xs_cur[:, 3:515], psums[dg])
                    if tb == 0:
                        pass
                    sc.drain()
                    sc.nop().then_inc(ACTS, 1)
                P.on("scalar", cvev)
                a_cv = P.inc("ACTS", 1)
                P.cvev_last = a_cv

                def conv(v, dg=dg, a_cv=a_cv, xs_cur=xs_cur, xs_prv=xs_prv, tb=tb):
                    v.wait_ge(ACTS, a_cv)
                    if "tbcv" in skip:
                        v.engine_nop().then_inc(VEC, 1)
                        return
                    if tb == 0:
                        v.memset(xs_cur[:, 0:3], 0.0)
                    else:
                        v.tensor_copy(xs_cur[:, 0:3], xs_prv[:, 512:515])
                    v.drain()
                    v.tensor_scalar(s_tmp[:, 0:512], xs_cur[:, 0:512],
                                    m_cols[:, dg * 4:dg * 4 + 1], None, ALU.mult)
                    v.scalar_tensor_tensor(s_tmp[:, 0:512], xs_cur[:, 1:513],
                                           m_cols[:, dg * 4 + 1:dg * 4 + 2], s_tmp[:, 0:512],
                                           ALU.mult, ALU.add)
                    v.scalar_tensor_tensor(s_tmp[:, 0:512], xs_cur[:, 2:514],
                                           m_cols[:, dg * 4 + 2:dg * 4 + 3], s_tmp[:, 0:512],
                                           ALU.mult, ALU.add)
                    v.scalar_tensor_tensor(s_tmp[:, 0:512], xs_cur[:, 3:515],
                                           m_cols[:, dg * 4 + 3:dg * 4 + 4], s_tmp[:, 0:512],
                                           ALU.mult, ALU.add)
                    v.drain()
                    v.engine_nop().then_inc(VEC, 1)
                P.on("vector", conv)
                v_cv = P.inc("VEC", 1)

                def csil(sc, dg=dg, v_cv=v_cv):
                    sc.wait_ge(VEC, v_cv)
                    sc.activation(m_xc[:, dg * 512:(dg + 1) * 512], s_tmp[:, 0:512],
                                  AF.Silu, bias=m_cols[:, 8 + dg:9 + dg], scale=1.0).then_inc(ACTS, 1)
                P.on("scalar", csil)
                a_cs = P.inc("ACTS", 1)

                def cstore(g, dg=dg, tb=tb, a_cs=a_cs):
                    g.wait_ge(ACTS, a_cs)
                    if "tbc" in skip:
                        g.dma_start(out=xc_scr.ap()[0:1, 0:8], in_=m_xc[0:1, 0:8]).then_inc(ST, 16)
                        return
                    g.dma_start(out=bass.AP(xc_scr.ap().tensor, dg * 128 * L + tb * 512,
                                            [[L, 128], [1, 512]]),
                                in_=m_xc[:, dg * 512:(dg + 1) * 512]).then_inc(ST, 16)
                P.on("gpsimd", cstore)
                P.inc("ST", 16)

                def cbf(v, dg=dg, a_cs=a_cs):
                    v.wait_ge(ACTS, a_cs)
                    v.tensor_copy(m_xcb[:, dg * 512:(dg + 1) * 512],
                                  m_xc[:, dg * 512:(dg + 1) * 512])
                    v.drain()
                    v.engine_nop().then_inc(VEC, 1)
                P.on("vector", cbf)
                P.inc("VEC", 1)
            v_cbf = P.val("VEC")
            P.evac_marks[tb] = P.cvev_last

            def mmxp(t, v_cbf=v_cbf, tb=tb):
                t.wait_ge(VEC, v_cbf)
                if tb > 0:
                    t.wait_ge(ACTS, P.xpev_marks[tb - 1])
                if "tbxp" in skip:
                    t.nop().then_inc(TEN, 1)
                    return
                t.matmul(ps_xp[:, :], m_xpw[:, 0:33], m_xcb[:, 0:512],
                         start=True, stop=False)
                t.matmul(ps_xp[:, :], m_xpw[:, 33:66], m_xcb[:, 512:1024],
                         start=False, stop=True).then_inc(TEN, 1)
            P.on("tensor", mmxp)
            tv_xp = P.inc("TEN", 1)

            def xpev(sc, tv_xp=tv_xp):
                sc.wait_ge(TEN, tv_xp)
                sc.copy(m_xpst[:, :], ps_xp[:, :])
                sc.drain()
                sc.nop().then_inc(ACTS, 1)
            P.on("scalar", xpev)
            a_xp = P.inc("ACTS", 1)
            P.xpev_marks[tb] = a_xp

            def xpst(g, tb=tb, a_xp=a_xp):
                g.wait_ge(ACTS, a_xp)
                if "tbxs" in skip:
                    g.dma_start(out=xp_ar_in.ap()[0:1, 0:8], in_=m_xpst[0:1, 0:8]).then_inc(ST, 16)
                    return
                g.dma_start(out=bass.AP(xp_ar_in.ap().tensor, tb * 512,
                                        [[L, 33], [1, 512]]),
                            in_=m_xpst[:, :]).then_inc(ST, 16)
            P.on("gpsimd", xpst)
            xp_st = P.inc("ST", 16)

        def ar_xp(g, xp_st=xp_st):
            g.wait_ge(ST, xp_st)
            if "ar" in skip:
                g.nop().then_inc(CC, 1)
                return
            g.collective_compute("AllReduce", ALU.add,
                                 replica_groups=[list(range(NCORES))],
                                 ins=[xp_ar_in.ap().opt()],
                                 outs=[xp_ar_out.ap().opt()]).then_inc(CC, 1)
        P.on("gpsimd", ar_xp)
        cc_ar = P.inc("CC", 1)

        ld_dtr = P.val("LD")

        if dev:
            dv["xp_dev"] = nc.dram_tensor("xp_dev", [33, L], F32, kind="ExternalOutput").ap()
            dv["xc_dev"] = nc.dram_tensor("xc_dev", [DI, L], F32, kind="ExternalOutput").ap()
            dv["z_dev"] = nc.dram_tensor("z_dev", [DI, L], F32, kind="ExternalOutput").ap()

            def p1tap(g):
                g.wait_ge(CC, cc_ar)
                g.dma_start(out=dv["xp_dev"], in_=xp_ar_out.ap()[:, :]).then_inc(ST, 16)
                g.dma_start(out=dv["xc_dev"], in_=xc_scr.ap()[:, :]).then_inc(ST, 16)
                g.dma_start(out=dv["z_dev"], in_=z_scr.ap()[:, :]).then_inc(ST, 16)
            P.on("gpsimd", p1tap)
            P.inc("ST", 48)
        if stage == "pass1":
            fin0 = P.val("ST")

            def fw0(g):
                g.wait_ge(ST, fin0)
                g.dma_start(out=out[0:128, :], in_=gout_b[:, :]).then_inc(ST, 16)
                g.wait_ge(ST, fin0 + 16)
            P.on("gpsimd", fw0)
            P.inc("ST", 16)

        if dev:
            dv["scan_dev"] = nc.dram_tensor("scan_dev", [128, 16384], F32, kind="ExternalOutput").ap()

        ag_yz_st = 0
        P.vec_done = {}
        P.ten_done = {}
        P.v_scn_last = 0
        P.a_dA_last = 0
        P.pending_extract = None
        for blk in (range(NSB) if stage != "pass1" else []):
            bbuf = blk % 2
            bco = bbuf * 33 * STB
            pv2 = P.vec_done.get(blk - 2, 0)
            pt2 = P.ten_done.get(blk - 2, 0)

            def bcload(e, blk=blk, bco=bco, pv2=pv2, pt2=pt2, bbuf=bbuf):
                e.wait_ge(CC, cc_ar)
                if blk >= 2:
                    e.wait_ge(VEC, pv2)
                    e.wait_ge(TEN, pt2)
                for dg in range(2):
                    e.dma_start(out=s_xc2[:, (bbuf * 2 + dg) * STB:(bbuf * 2 + dg + 1) * STB],
                                in_=bass.AP(xc_scr.ap().tensor, dg * 128 * L + blk * STB,
                                            [[L, 128], [1, STB]])).then_inc(LD3, 16)
                    e.dma_start(out=s_z2[:, (bbuf * 2 + dg) * STB:(bbuf * 2 + dg + 1) * STB],
                                in_=bass.AP(z_scr.ap().tensor, dg * 128 * L + blk * STB,
                                            [[L, 128], [1, STB]])).then_inc(LD3, 16)
                for p in range(16):
                    e.dma_start(out=s_bc[p:p + 1, bco:bco + 33 * STB],
                                in_=bass.AP(xp_ar_out.ap().tensor, blk * STB,
                                            [[0, 1], [L, 33], [1, STB]])).then_inc(LD3, 16)
                n = 16
                base = P.bcv[blk]
                step = 0
                while n < 128:
                    m = min(n, 128 - n)
                    e.wait_ge(LD3, base + 64 + 256 + step * 16)
                    e.dma_start(out=s_bc[n:n + m, bco:bco + 33 * STB],
                                in_=s_bc[0:m, bco:bco + 33 * STB]).then_inc(LD3, 16)
                    step += 1
                    n += m
            if not hasattr(P, "bcv"):
                P.bcv = {}
            P.bcv[blk] = P.val("LD3")
            P.on("sync", bcload)
            P.inc("LD3", 64 + 16 * 16 + 16 * 3)
            ld_blk = P.val("LD3")

            prev = P.val("ACTS")

            ten_prev_blk = P.val("TEN")

            if dev and blk == 0:
                def earlytap(g, ld_blk=ld_blk):
                    g.wait_ge(LD3, ld_blk)
                    g.dma_start(out=dv["scan_dev"][:, 6656:6912],
                                in_=s_bc[:, 32 * STB:33 * STB]).then_inc(ST, 16)
                    g.dma_start(out=dv["scan_dev"][:, 7168:7680],
                                in_=s_xc2[:, 0:512]).then_inc(ST, 16)
                P.on("gpsimd", earlytap)
                et_st = P.inc("ST", 32)

                def holdtap(v, et_st=et_st):
                    v.wait_ge(ST, et_st)
                P.on("vector", holdtap)

            tapst = getattr(P, "scantap_st", {}).get(blk - 1, 0) if dev else 0

            def dtc1(v, ld_blk=ld_blk, bco=bco, pt2=pt2, tapst=tapst,
                     a_prev=P.a_dA_last):
                v.wait_ge(LD3, ld_blk)
                if a_prev:
                    v.wait_ge(ACTS, a_prev)
                v.wait_ge(TEN, pt2)
                if tapst:
                    v.wait_ge(ST, tapst)
                for dg in range(2):
                    v.tensor_scalar(s_dt[:, dg * STB:(dg + 1) * STB],
                                    s_bc[:, bco:bco + STB],
                                    m_cols[:, 12 + dg:13 + dg],
                                    m_cols[:, 14 + dg:15 + dg],
                                    ALU.mult, ALU.add)
                v.drain()
                v.tensor_scalar(s_tmp[:, 0:512].bitcast(I32), s_dt[:, 0:512].bitcast(I32),
                                maskc[:, 0:1], None, ALU.bitwise_and)
                v.engine_nop().then_inc(VEC, 1)
            P.on("vector", dtc1)
            v_d1 = P.inc("VEC", 1)

            def dtc2(sc, v_d1=v_d1):
                sc.wait_ge(VEC, v_d1)
                sc.activation(s_tmp[:, 0:512], s_tmp[:, 0:512], AF.Exp, bias=0.0, scale=-1.0)
                sc.activation(s_tmp[:, 0:512], s_tmp[:, 0:512], AF.Ln,
                              bias=fconst[:, 1:2], scale=1.0)
                sc.activation(s_U[:, 0:512], s_dt[:, 0:512], AF.Relu, bias=0.0, scale=1.0)
                sc.drain()
                sc.nop().then_inc(ACTS, 1)
            P.on("scalar", dtc2)
            a_d2 = P.inc("ACTS", 1)

            def dtc3(v, a_d2=a_d2, bbuf=bbuf):
                v.wait_ge(ACTS, a_d2)
                v.tensor_tensor(s_dt[:, 0:512], s_U[:, 0:512], s_tmp[:, 0:512], ALU.add)
                v.tensor_tensor(s_U[:, 0:512], s_dt[:, 0:512],
                                s_xc2[:, bbuf * 512:(bbuf + 1) * 512], ALU.mult)
                v.engine_nop().then_inc(VEC, 1)
            P.on("vector", dtc3)
            v_d3 = P.inc("VEC", 1)

            for dg in range(2):
                ten_tpy_prev = P.val("TEN")

                def dAe(sc, dg=dg, v_d3=v_d3, prev_scn=P.v_scn_last):
                    sc.wait_ge(VEC, v_d3)
                    if prev_scn:
                        sc.wait_ge(VEC, prev_scn)
                    for n in range(N_ST):
                        sc.activation(s_dA[:, n * STB:(n + 1) * STB],
                                      s_dt[:, dg * STB:(dg + 1) * STB], AF.Exp,
                                      bias=0.0, scale=s_A[:, dg * 16 + n:dg * 16 + n + 1])
                    sc.nop().then_inc(ACTS, 1)
                P.on("scalar", dAe)
                a_dA = P.inc("ACTS", 1)
                P.a_dA_last = a_dA

                def scn(v, dg=dg, blk=blk, a_dA=a_dA, bco=bco, bbuf=bbuf,
                        ten_tpy_prev=ten_tpy_prev, pend=P.pending_extract):
                    prow = s_U.ap().ap[0][0]
                    ub = bass.AP(s_U.ap().tensor, dg * STB, [[prow, 128], [0, 16], [1, STB]])
                    bv = bass.AP(s_bc.ap().tensor, bco + STB, [[s_bc.ap().ap[0][0], 128], [STB, 16], [1, STB]])
                    dxv = bass.AP(s_dBx.ap().tensor, 0, [[s_dBx.ap().ap[0][0], 128], [STB, 16], [1, STB]])
                    v.tensor_tensor(dxv, ub, bv, ALU.mult)
                    if pend is not None:
                        hl = bass.AP(s_h.ap().tensor, STB - 1, [[s_h.ap().ap[0][0], 128], [STB, 16]])
                        v.tensor_copy(s_hprev[:, pend * 16:(pend + 1) * 16], hl)
                    v.wait_ge(ACTS, a_dA)
                    for n in range(N_ST):
                        init = 0.0 if blk == 0 else s_hprev[:, dg * 16 + n:dg * 16 + n + 1]
                        v.tensor_tensor_scan(s_h[:, n * STB:(n + 1) * STB],
                                             s_dA[:, n * STB:(n + 1) * STB],
                                             s_dBx[:, n * STB:(n + 1) * STB],
                                             init, ALU.mult, ALU.add)
                    v.engine_nop().then_inc(VEC, 1)
                    v.wait_ge(TEN, ten_tpy_prev)
                    cv = bass.AP(s_bc.ap().tensor, bco + 17 * STB,
                                 [[s_bc.ap().ap[0][0], 128], [STB, 16], [1, STB]])
                    hv = bass.AP(s_h.ap().tensor, 0, [[s_h.ap().ap[0][0], 128], [STB, 16], [1, STB]])
                    hcv = bass.AP(s_hc.ap().tensor, 0, [[s_hc.ap().ap[0][0], 128], [STB, 16], [1, STB]])
                    v.tensor_tensor(hcv, hv, cv, ALU.mult)
                    v.tensor_scalar(s_xcd[:, (bbuf * 2 + dg) * STB:(bbuf * 2 + dg + 1) * STB],
                                    s_xc2[:, (bbuf * 2 + dg) * STB:(bbuf * 2 + dg + 1) * STB],
                                    m_cols[:, 10 + dg:11 + dg], None, ALU.mult)
                    v.engine_nop().then_inc(VEC, 1)
                P.on("vector", scn)
                P.inc("VEC", 1)
                v_sc = P.inc("VEC", 1)
                P.v_scn_last = v_sc - 1
                P.pending_extract = dg

                if dev and blk < 2 and dg == 0:
                    P.scantap_st = getattr(P, "scantap_st", {})

                    def scantap(g, blk=blk, v_sc=v_sc):
                        g.wait_ge(VEC, v_sc)
                        g.dma_start(out=dv["scan_dev"][:, blk * 8192:blk * 8192 + 512],
                                    in_=s_dt[:, 0:512]).then_inc(ST, 16)
                        g.dma_start(out=dv["scan_dev"][:, blk * 8192 + 512:blk * 8192 + 4608],
                                    in_=s_h[:, :]).then_inc(ST, 16)
                        g.dma_start(out=dv["scan_dev"][:, blk * 8192 + 4608:blk * 8192 + 4608 + 512],
                                    in_=s_U[:, 0:512]).then_inc(ST, 16)
                        g.dma_start(out=dv["scan_dev"][:, blk * 8192 + 5120:blk * 8192 + 5120 + 512],
                                    in_=s_bc[:, (blk % 2) * 32 * STB:(blk % 2) * 32 * STB + 512]).then_inc(ST, 16)
                        g.dma_start(out=dv["scan_dev"][:, blk * 8192 + 5632:blk * 8192 + 5632 + 256],
                                    in_=s_bc[:, (blk % 2) * 33 * STB + 32 * STB:
                                             (blk % 2) * 33 * STB + 33 * STB]).then_inc(ST, 16)
                        g.dma_start(out=dv["scan_dev"][:, blk * 8192 + 6144:blk * 8192 + 6144 + 512],
                                    in_=s_bc[:, 0:512]).then_inc(ST, 16)
                    P.on("gpsimd", scantap)
                    P.scantap_st[blk] = P.inc("ST", 96)

                for ch in range(STB // 128):
                    def tpy(t, dg=dg, ch=ch, v_sc=v_sc, blk=blk):
                        t.wait_ge(VEC, v_sc)
                        last = None
                        for n in range(N_ST):
                            last = t.matmul(ps_a[:, ch * 256 + dg * 128: ch * 256 + (dg + 1) * 128],
                                            s_hc[:, n * STB + ch * 128: n * STB + (ch + 1) * 128],
                                            ident[:, :], is_transpose=True,
                                            start=(dg == 0 and ch == 0 and n == 0),
                                            stop=(n == N_ST - 1), skip_group_check=True)
                        last.then_inc(TEN, 1)
                    P.on("tensor", tpy)
                    P.inc("TEN", 1)
            v_g1 = P.val("VEC")

            for ch in range(STB // 128):
                prev_bev = P.val("ACTS")

                def tpxz(t, ch=ch, v_g1=v_g1, prev_bev=prev_bev, bbuf=bbuf):
                    t.wait_ge(VEC, v_g1)
                    t.wait_ge(ACTS, prev_bev)
                    last = None
                    for dg in range(2):
                        t.matmul(ps_a[:, ch * 256 + dg * 128: ch * 256 + (dg + 1) * 128],
                                 s_xcd[:, (bbuf * 2 + dg) * STB + ch * 128:
                                       (bbuf * 2 + dg) * STB + (ch + 1) * 128],
                                 ident[:, :], is_transpose=True, start=False, stop=True,
                                 skip_group_check=True)
                        last = t.matmul(ps_big[:, ch * 256 + dg * 128: ch * 256 + (dg + 1) * 128],
                                        s_z2[:, (bbuf * 2 + dg) * STB + ch * 128:
                                             (bbuf * 2 + dg) * STB + (ch + 1) * 128],
                                        ident[:, :], is_transpose=True,
                                        start=(ch == 0 and dg == 0), stop=True,
                                        skip_group_check=True)
                    last.then_inc(TEN, 1)
                P.on("tensor", tpxz)
                tv_xz = P.inc("TEN", 1)

                def gsl(sc, ch=ch, tv_xz=tv_xz):
                    sc.wait_ge(TEN, tv_xz)
                    sc.activation(s_zs[:, :], ps_big[:, ch * 256:ch * 256 + 256], AF.Silu,
                                  bias=0.0, scale=1.0)
                    sc.drain()
                    sc.nop().then_inc(ACTS, 1)
                P.on("scalar", gsl)
                a_gs = P.inc("ACTS", 1)

                def gat2(v, ch=ch, a_gs=a_gs):
                    v.wait_ge(ACTS, a_gs)
                    v.tensor_tensor(s_yz[:, :], ps_a[:, ch * 256:ch * 256 + 256],
                                    s_zs[:, :], ALU.mult)
                    v.drain()
                    v.engine_nop().then_inc(VEC, 1)
                P.on("vector", gat2)
                v_g2 = P.inc("VEC", 1)

                def tpb(t, ch=ch, v_g2=v_g2):
                    t.wait_ge(VEC, v_g2)
                    last = None
                    for dg in range(2):
                        last = t.matmul(ps_big[:, 512 + ch * 256 + dg * 128:
                                               512 + ch * 256 + (dg + 1) * 128],
                                        s_yz[:, dg * 128:(dg + 1) * 128],
                                        ident[:, :], is_transpose=True,
                                        start=(ch == 0 and dg == 0), stop=True,
                                        skip_group_check=True)
                    last.then_inc(TEN, 1)
                P.on("tensor", tpb)
                tv_b = P.inc("TEN", 1)

                def bev(sc, ch=ch, tv_b=tv_b):
                    sc.wait_ge(TEN, tv_b)
                    sc.copy(s_yzT[:, 0 * STB + ch * 128: 0 * STB + (ch + 1) * 128],
                            ps_big[:, 512 + ch * 256:512 + ch * 256 + 128])
                    sc.copy(s_yzT[:, 1 * STB + ch * 128: 1 * STB + (ch + 1) * 128],
                            ps_big[:, 512 + ch * 256 + 128:512 + ch * 256 + 256])
                    sc.drain()
                    sc.nop().then_inc(ACTS, 1)
                P.on("scalar", bev)
                P.inc("ACTS", 1)
            a_bev = P.val("ACTS")

            if dev and blk == 0:
                def yztap(g, a_bev=a_bev):
                    g.wait_ge(ACTS, a_bev)
                    g.dma_start(out=dv["scan_dev"][:, 7680:7936],
                                in_=s_yz[:, 0:256]).then_inc(ST, 16)
                    g.dma_start(out=dv["scan_dev"][:, 7936:8192],
                                in_=s_zs[:, 0:256]).then_inc(ST, 16)
                    g.dma_start(out=dv["scan_dev"][:, 14848:15360],
                                in_=s_yzT.ap().bitcast(F32)[:, 0:512] if False else s_tmp[:, 0:512]).then_inc(ST, 16)
                P.on("gpsimd", yztap)
                P.scantap_st[0] = P.inc("ST", 48)

            def yst(g, blk=blk, a_bev=a_bev):
                g.wait_ge(ACTS, a_bev)
                for dg in range(2):
                    g.dma_start(out=bass.AP(yz_ag_inq[blk // 8].ap().tensor,
                                            dg * 128 * (L // 8) + (blk % 8) * STB,
                                            [[L // 8, 128], [1, STB]]),
                                in_=s_yzT[:, dg * STB:(dg + 1) * STB]).then_inc(ST, 16)
            P.on("gpsimd", yst)
            ag_yz_st = P.inc("ST", 32)
            if stage in ("mid", "full") and blk % 8 == 7:
                q = blk // 8

                def ag_yz_q(g, q=q, st_q=ag_yz_st):
                    g.wait_ge(ST, st_q)
                    g.collective_compute(
                        "AllGather", ALU.bypass,
                        replica_groups=[list(range(NCORES))],
                        ins=[yz_ag_inq[q].ap().opt()],
                        outs=[yz_ag_out.ap()[q:q + 1, :].opt()],
                    ).then_inc(CC, 1)
                P.on("gpsimd", ag_yz_q)
                P.inc("CC", 1)
            P.vec_done[blk] = P.val("VEC")
            P.ten_done[blk] = P.val("TEN")

        if dev and stage in ("scan", "mid", "full"):
            dv["yzT_dev"] = nc.dram_tensor("yzT_dev", [DI, L], BF16, kind="ExternalOutput").ap()
            lastst = P.val("ST")

            def p2tap(g, lastst=lastst):
                g.wait_ge(ST, lastst)
                g.dma_start(out=dv["yzT_dev"], in_=yz_ag_in.ap()[:, :]).then_inc(ST, 16)
            P.on("gpsimd", p2tap)
            P.inc("ST", 16)

        if stage in ("pass1", "scan"):
            finp = P.val("ST")

            def fwp(g, finp=finp):
                g.wait_ge(ST, finp)
                g.dma_start(out=out[0:128, :], in_=gout_b[:, :]).then_inc(ST, 16)
                g.wait_ge(ST, finp + 16)
            P.on("gpsimd", fwp)
            P.inc("ST", 16)

        do_out = stage in ("mid", "full")
        if do_out:
            cc_yz = P.val("CC")

        vec_scan_end = P.val("VEC")
        pid_tok0 = []
        for di in (range(16) if do_out else []):
            def lwo(e, di=di, vec_scan_end=vec_scan_end):
                if di == 0:
                    e.wait_ge(VEC, vec_scan_end)
                e.dma_start(out=o_t1[:, :], in_=w_outT_in[di * 128:(di + 1) * 128, :]).then_inc(LD, 16)
            P.on("sync", lwo)
            ldv = P.inc("LD", 16)

            def cwo(v, di=di, ldv=ldv):
                v.wait_ge(LD, ldv)
                v.tensor_copy(o_wout[:, di * 1024:(di + 1) * 1024], o_t1[:, :])
                v.drain()
                v.engine_nop().then_inc(VEC, 1)
            P.on("vector", cwo)
            P.inc("VEC", 1)
            prev_cast = P.val("VEC")

            def pace(e, prev_cast=prev_cast):
                e.wait_ge(VEC, prev_cast)
            P.on("sync", pace)
        v_wo = P.val("VEC")

        def lyz_all(e):
            if not do_out:
                return
            e.wait_ge(CC, cc_yz)
            e.wait_ge(VEC, vec_scan_end)
            kb = e.compute_val(e.partition_id() * 1)
            for di in range(16):
                e.dma_start(out=o_yzT[:, di * TOK:(di + 1) * TOK],
                            in_=yz_ag_out.ap()[bass.ds(kb, 1),
                                               di * 128 * TOK:(di + 1) * 128 * TOK]).then_inc(LD2, 16)
        if do_out:
            P.on("gpsimd", lyz_all)
            ld_yz_all = P.inc("LD2", 16 * 16)

        for tt in (range(NTT) if do_out else []):
            prev_fin = P.val("VEC")

            def lyz(e, tt=tt, prev_fin=prev_fin):
                if tt >= 1:
                    e.wait_ge(VEC, prev_fin)
                e.dma_start(out=o_x[:, :], in_=x_in[tt * 128:(tt + 1) * 128, :]).then_inc(LD2, 16)
                e.dma_start(out=o_g[:, :],
                            in_=bass.AP(g_scr.ap().tensor, tt * 128 * D_MODEL,
                                        [[D_MODEL, 128], [1, D_MODEL]])).then_inc(LD2, 16)
            P.on("gpsimd", lyz)
            ldv = P.inc("LD2", 16 * 2)

            mmo_gate = P.val("VEC")

            def mmo(t, ldv=ldv, v_wo=v_wo, tt=tt, mmo_gate=mmo_gate):
                t.wait_ge(LD2, max(ldv, ld_yz_all))
                t.wait_ge(VEC, v_wo)
                if tt > 0:
                    t.wait_ge(VEC, mmo_gate)
                last = None
                for half in range(2):
                    for di in range(16):
                        last = t.matmul(ps_big[:, half * 512:(half + 1) * 512],
                                        o_yzT[:, di * TOK + tt * 128: di * TOK + (tt + 1) * 128],
                                        o_wout[:, di * 1024 + half * 512: di * 1024 + (half + 1) * 512],
                                        start=(di == 0), stop=(di == 15))
                last.then_inc(TEN, 1)
            P.on("tensor", mmo)
            tv_o = P.inc("TEN", 1)

            def fin(v, tv_o=tv_o, tt=tt):
                v.wait_ge(TEN, tv_o)
                v.tensor_tensor(o_t1[:, :], ps_big[:, :], o_g[:, :], ALU.mult)
                v.drain()
                v.tensor_tensor(o_t1[:, :], o_t1[:, :], o_x[:, :], ALU.add)
                v.drain()
                v.engine_nop().then_inc(VEC, 1)
            P.on("vector", fin)
            v_fin = P.inc("VEC", 1)

            def outst(g, tt=tt, v_fin=v_fin):
                g.wait_ge(VEC, v_fin)
                g.dma_start(out=out[tt * 128:(tt + 1) * 128, :],
                            in_=o_t1[:, :]).then_inc(ST, 16)
            P.on("gpsimd", outst)
            P.inc("ST", 16)
        if do_out:
            fin_st = P.val("ST")

            def waitall(g):
                g.wait_ge(ST, fin_st)
            P.on("gpsimd", waitall)

    if stage == "ffn":
        def fin_cp(e):
            e.wait_ge(ST, g_all)
            e.wait_ge(CC, cc_xn)
            for i in range(NTT):
                e.dma_start(out=out[i * 128:(i + 1) * 128, :],
                            in_=bass.AP(g_scr.ap().tensor, i * 128 * D_MODEL,
                                        [[D_MODEL, 128], [1, D_MODEL]])).then_inc(ST, 16)
        P.on("sync", fin_cp)
        fin = P.inc("ST", 16 * NTT)

        def fin_wait(g):
            g.wait_ge(ST, fin)
        P.on("gpsimd", fin_wait)

    with nc.Block() as block:
        @block.sync
        def _(e):
            for f in P.q["sync"]:
                f(e)

        @block.scalar
        def _(e):
            for f in P.q["scalar"]:
                f(e)

        @block.vector
        def _(e):
            for f in P.q["vector"]:
                f(e)

        @block.tensor
        def _(e):
            for f in P.q["tensor"]:
                f(e)

        @block.gpsimd
        def _(e):
            for f in P.q["gpsimd"]:
                f(e)
    ctx.close()
    return nc


def shard_inputs(inputs):
    x = np.ascontiguousarray(np.asarray(inputs["L0_attn_x"]).reshape(L, D_MODEL))
    wg = np.asarray(inputs["L0_ffn_w_gate"])
    wu = np.asarray(inputs["L0_ffn_w_up"])
    wd = np.asarray(inputs["L0_ffn_w_down"])
    ident = np.eye(128, dtype=np.float32)
    mask = np.zeros((128, 3), np.int32)
    mask[:, 0] = 0x7FFFFFFF
    mask[:, 1] = np.int32(np.uint32(0x80000000).view(np.int32))
    mask[:, 2] = 0x3F800000
    fc = np.zeros((128, 4), np.float32)
    fc[:, 0] = EPS
    fc[:, 1] = 1.0
    fc[:, 2] = -1.0
    fc[:, 3] = 0.5
    w_in = np.asarray(inputs["L0_attn_in_proj_w"])
    conv_w = np.asarray(inputs["L0_attn_conv_w"])[:, 0, :]
    conv_b = np.asarray(inputs["L0_attn_conv_b"])
    xpw = np.asarray(inputs["L0_attn_x_proj_w"])
    dtw = np.asarray(inputs["L0_attn_dt_proj_w"])[:, 0]
    dtb = np.asarray(inputs["L0_attn_dt_proj_b"])
    alog = np.asarray(inputs["L0_attn_A_log"])
    dvec = np.asarray(inputs["L0_attn_D"])
    w_out = np.asarray(inputs["L0_attn_out_proj_w"])
    w_outT = np.ascontiguousarray(w_out.T)

    def colpair(vec, k):
        return np.ascontiguousarray(vec[k * DI:(k + 1) * DI].reshape(2, 128).T)

    maps = []
    for k in range(NCORES):
        winT = np.ascontiguousarray(
            np.concatenate([w_in[k * DI:(k + 1) * DI],
                            w_in[D_INNER + k * DI:D_INNER + (k + 1) * DI]], 0).T)
        maps.append({
            "x_in": np.ascontiguousarray(x[k * TOK:(k + 1) * TOK]),
            "ident_in": ident,
            "mask_in": mask,
            "fc_in": fc,
            "wg_sl": np.ascontiguousarray(wg[k * IGR:(k + 1) * IGR]),
            "wu_sl": np.ascontiguousarray(wu[k * IGR:(k + 1) * IGR]),
            "wd_sl": np.ascontiguousarray(wd[k * DGR:(k + 1) * DGR]),
            "w_inT_sl": winT,
            "xpw_sl": np.ascontiguousarray(xpw.T[k * DI:(k + 1) * DI]),
            "conv_sl": np.ascontiguousarray(conv_w[k * DI:(k + 1) * DI]),
            "convb_sl": colpair(conv_b, k),
            "d_sl": colpair(dvec, k),
            "dtw_sl": colpair(dtw, k),
            "dtb_sl": colpair(dtb, k),
            "alog_sl": np.ascontiguousarray(alog[k * DI:(k + 1) * DI]),
            "w_outT_in": w_outT,
        })
    return maps


def kernel(**inputs):
    nc = build(stage="full", dev=False)
    maps = shard_inputs(inputs)
    res = run_bass_kernel_spmd(nc, maps, core_ids=list(range(NCORES)))
    outp = np.concatenate([r["out"] for r in res.results], axis=0)
    return outp.reshape(1, L, D_MODEL)
